# version 32
# baseline (speedup 1.0000x reference)
import numpy as np
import ml_dtypes
import concourse.bass as bass
import concourse.mybir as mybir
from concourse.bass_utils import run_bass_kernel_spmd
from contextlib import ExitStack

F32 = mybir.dt.float32
BF16 = mybir.dt.bfloat16
ALU = mybir.AluOpType
AF = mybir.ActivationFunctionType
AX = mybir.AxisListType

MAGIC = float(np.float32(1.5 * 2.0 ** 23))
NEG = -30080.0
EPS = 1e-5
D = 1024
FI = 4096
TOK = 512
NSUB = 4
NGRP = TOK // 8
NBAT = NGRP // 4
NPC = 24
N_CORES = 8


def _consts():
    ident = np.eye(128, dtype=ml_dtypes.bfloat16)
    swap = np.zeros((128, 128), dtype=ml_dtypes.bfloat16)
    for i in range(128):
        swap[(i + 64) % 128, i] = 1.0
    mask = np.full((128, 512), NEG, dtype=ml_dtypes.bfloat16)
    r = np.arange(128) % 8
    c = np.arange(512) % 8
    mask[r[:, None] == c[None, :]] = 0.0
    ones_row = np.ones((1, 128), dtype=ml_dtypes.bfloat16)
    ones_col = np.ones((128, 1), dtype=np.float32)
    return ident, swap, mask, ones_row, ones_col


def build_nc(nt):
    n_tok = nt * TOK
    nc = bass.Bass("TRN2", target_bir_lowering=False)

    x_h = nc.dram_tensor("x", [n_tok, D], F32, kind="ExternalInput").ap()
    wqkvT_h = nc.dram_tensor("wqkvT", [D, 3 * D], F32, kind="ExternalInput").ap()
    wprojT_h = nc.dram_tensor("wprojT", [D, D], F32, kind="ExternalInput").ap()
    bproj_h = nc.dram_tensor("bproj", [1, D], F32, kind="ExternalInput").ap()
    wgT_h = nc.dram_tensor("wgT", [D, FI], F32, kind="ExternalInput").ap()
    wuT_h = nc.dram_tensor("wuT", [D, FI], F32, kind="ExternalInput").ap()
    wdT_h = nc.dram_tensor("wdT", [FI, D], F32, kind="ExternalInput").ap()
    out_h = nc.dram_tensor("out", [n_tok, D], F32, kind="ExternalOutput").ap()

    xnew_h = nc.dram_tensor("xnew_scr", [n_tok, D], F32).ap()
    hq_h = nc.dram_tensor("hq_scr", [nt * NSUB, 128, FI], BF16).ap()
    wgq_h = nc.dram_tensor("wgq_scr", [D, FI], BF16).ap()
    wuq_h = nc.dram_tensor("wuq_scr", [D, FI], BF16).ap()
    wdq_h = nc.dram_tensor("wdq_scr", [FI, D], BF16).ap()

    ident_np, swap_np, mask_np, ones_row_np, ones_col_np = _consts()
    ident_h = nc.inline_tensor(ident_np, name="c_ident_h").ap()
    swap_h = nc.inline_tensor(swap_np, name="c_swap_h").ap()
    mask_h = nc.inline_tensor(mask_np, name="c_mask_h").ap()
    onesr_h = nc.inline_tensor(ones_row_np, name="c_onesr_h").ap()
    onesc_h = nc.inline_tensor(ones_col_np, name="c_onesc_h").ap()
    onesrf_h = nc.inline_tensor(np.ones((1, 128), dtype=np.float32), name="c_onesrf_h").ap()

    def piece_desc(j):
        if j < 8:
            return wgT_h[j * 128:(j + 1) * 128, :], wgq_h[j * 128:(j + 1) * 128, :], FI
        if j < 16:
            k = j - 8
            return wuT_h[k * 128:(k + 1) * 128, :], wuq_h[k * 128:(k + 1) * 128, :], FI
        k = j - 16
        src = wdT_h[k * 512:(k + 1) * 512, :].rearrange("(s p) f -> p s f", p=128)
        dst = wdq_h[k * 512:(k + 1) * 512, :].rearrange("(s p) f -> p s f", p=128)
        return src, dst, FI

    def sw_col(j):
        return 0 if j < 8 else (1 if j < 16 else 2)

    with ExitStack() as ctx:
        e = ctx.enter_context
        c_id = e(nc.sbuf_tensor("c_id", [128, 128], BF16))
        c_sw = e(nc.sbuf_tensor("c_sw", [128, 128], BF16))
        c_1r = e(nc.sbuf_tensor("c_1r", [1, 128], BF16))
        c_1c = e(nc.sbuf_tensor("c_1c", [128, 1], F32))
        c_1rf = e(nc.sbuf_tensor("c_1rf", [1, 128], F32))
        wsc = e(nc.sbuf_tensor("wsc", [128, 4], F32))
        sums_sb = e(nc.sbuf_tensor("sums_sb", [1, 64], F32))
        acc_sb = e(nc.sbuf_tensor("acc_sb", [128, NPC], F32))
        gam = e(nc.sbuf_tensor("gam", [128, 64], F32))
        agm = e(nc.sbuf_tensor("agm", [128, 8], F32))
        lns = e(nc.sbuf_tensor("lns", [128, 72], F32))
        ln2 = e(nc.sbuf_tensor("ln2", [128, 96], F32))

        P = [e(nc.psum_tensor(f"pb{i}", [128, 512], F32)) for i in range(8)]

        sem_names = [
            "s_wa", "s_wag", "s_stg0", "s_stg1", "s_sg1", "s_st2a", "s_st2b", "s_qo0", "s_qo1", "s_x0", "s_x1", "s_wb1", "s_wb2",
            "s_xno0", "s_xno1", "s_xni0", "s_xni1", "s_xni2", "s_hqo0", "s_hqo1", "s_hqia", "s_hqib", "s_xnia", "s_xnib",
            "s_out0", "s_out1", "g_bc",
            "c_abs", "t_sums", "v_scales", "v_stg", "v_q",
            "v_st1", "c_sq", "v_ln", "c_xn", "t_xnT", "c_xnfm",
            "t_qkv", "c_qkvev", "t_ksw", "v_ksw",
            "t_S", "v_S2", "c_E", "v_A",
            "t_At", "c_Atev", "t_V8", "c_V8ev", "t_AV", "c_AVev",
            "t_pj", "v_pe",
            "v_st2", "c_sq2", "v_xq", "t_xqT", "c_xqfm",
            "t_g", "c_g", "t_u", "c_u", "v_M", "v_hq", "t_hqT", "c_hqfm",
            "t_d", "v_o",
        ]
        SM = {n: e(nc.semaphore(n)) for n in sem_names}
        DU = {}

        def dmax(eng, out, in_, semname):
            ins = eng.dma_start(out=out, in_=in_).then_inc(SM[semname], 16)
            tot = 0
            for u in ins.ins.sync.on_update:
                if u.id == SM[semname].num:
                    tot += u.update_value
            if semname in DU:
                assert DU[semname] == tot, (semname, DU[semname], tot)
            else:
                DU[semname] = tot
            return ins

        def du(semname):
            return DU[semname]

        s1f0 = e(nc.sbuf_tensor("s1f0", [128, 512], F32))
        esb0 = e(nc.sbuf_tensor("esb0", [128, 512], F32))
        asb0 = e(nc.sbuf_tensor("asb0", [128, 512], BF16))
        atsb0 = e(nc.sbuf_tensor("atsb0", [128, 512], BF16))
        v8sb0 = e(nc.sbuf_tensor("v8sb0", [128, 256], BF16))

        AB = (nc.sbuf_base + 31) // 32 * 32
        KB = 1024

        def at(name, shape, dtype, off):
            assert AB + off + _bytes(shape, dtype) <= 229344, (name, off)
            return nc.alloc_sbuf_tensor_at(name, shape, dtype, offset=AB + off)

        def _bytes(shape, dtype):
            esz = 4 if dtype == F32 else 2
            return shape[1] * esz

        wqkvT = at("wqkvT_sb", [128, 8 * 3072], BF16, 0)
        wprojT = at("wprojT_sb", [128, 8 * 1024], BF16, 48 * KB)
        x_sb = [at(f"x_sb{i}", [128, 4096], F32, 64 * KB + 16 * KB * i) for i in range(2)]
        xn_tm = at("xn_tm", [128, 4096], BF16, 96 * KB)
        xn_fm = at("xn_fm", [128, 4096], BF16, 104 * KB)
        attn_fm = at("attn_fm", [128, 4096], BF16, 112 * KB)
        qt = at("qt", [128, 4096], BF16, 120 * KB)
        kt = at("kt", [128, 4096], BF16, 128 * KB)
        vt = at("vt", [128, 4096], BF16, 136 * KB)
        ktx = at("ktx", [128, 4096], BF16, 144 * KB)
        xnew_sb = [at(f"xnew{i}", [128, 4096], F32, 152 * KB + 16 * KB * i) for i in range(2)]
        c_mask = at("c_mask", [128, 512], BF16, 190 * KB + 512)
        bproj = at("bproj_sb", [1, 1024], BF16, 196 * KB + 512)
        s1f1 = at("s1f1", [128, 512], F32, 184 * KB)
        esb1 = at("esb1", [128, 512], F32, 186 * KB)
        asb1 = at("asb1", [128, 512], BF16, 188 * KB)
        atsb1 = at("atsb1", [128, 512], BF16, 189 * KB)
        v8sb1 = at("v8sb1", [128, 256], BF16, 190 * KB)
        asb2 = at("asb2", [128, 512], BF16, 191 * KB + 512)
        asb3 = at("asb3", [128, 512], BF16, 192 * KB + 512)
        atsb2 = at("atsb2", [128, 512], BF16, 193 * KB + 512)
        atsb3 = at("atsb3", [128, 512], BF16, 194 * KB + 512)
        v8sb2 = at("v8sb2", [128, 256], BF16, 195 * KB + 512)
        v8sb3 = at("v8sb3", [128, 256], BF16, 196 * KB)
        s1f = [s1f0, s1f1]
        esb = [esb0, esb1]
        asb = [asb0, asb1, asb2, asb3]
        atsb = [atsb0, atsb1, atsb2, atsb3]
        v8sb = [v8sb0, v8sb1, v8sb2, v8sb3]
        stage = [at(f"stage{i}", [128, FI], F32, 120 * KB + 16 * KB * i) for i in range(2)]
        qtmp = at("qtmp", [128, FI], F32, 152 * KB)
        qout = [at(f"qout{i}", [128, FI], BF16, 168 * KB + 8 * KB * i) for i in range(2)]
        wgq = at("wgq_sb", [128, 8 * FI], BF16, 0)
        wuq = at("wuq_sb", [128, 8 * FI], BF16, 64 * KB)
        xnsub = [at(f"xnsub{i}", [128, 1024], F32, 128 * KB + 4 * KB * i) for i in range(2)]
        xqm = at("xqm", [128, 1024], F32, 136 * KB)
        xq_tm = [at(f"xq_tm{i}", [128, 1024], BF16, 140 * KB + 2 * KB * i) for i in range(3)]
        xq_fm = [at(f"xq_fm{i}", [128, 1024], BF16, 146 * KB + 2 * KB * i) for i in range(2)]
        silug = [at(f"silug{i}", [128, 512], F32, 150 * KB + 2 * KB * i) for i in range(2)]
        u_sb = [at(f"u_sb{i}", [128, 512], F32, 154 * KB + 2 * KB * i) for i in range(2)]
        m_sb = at("m_sb", [128, FI], F32, 158 * KB)
        hq_tm = at("hq_tm", [128, FI], BF16, 174 * KB)
        hq_fm = [at(f"hq_fm{i}", [128, FI], BF16, 182 * KB + 8 * KB * i) for i in range(2)]
        wdq = at("wdq_sb", [128, 32 * 1024], BF16, 0)
        hq_in = [at(f"hq_in{i}", [128, FI], BF16, 64 * KB + 8 * KB * i) for i in range(2)]
        xnsub2 = [at(f"xnsub2{i}", [128, 1024], F32, 80 * KB + 4 * KB * i) for i in range(2)]
        out_sb = [at(f"out_sb{i}", [128, 1024], F32, 88 * KB + 4 * KB * i) for i in range(2)]

        block = e(nc.Block())

        def sync_body(sync):
            sync.dma_start(out=c_id[:], in_=ident_h).then_inc(SM["s_wa"], 16)
            sync.dma_start(out=c_sw[:], in_=swap_h).then_inc(SM["s_wa"], 16)
            sync.dma_start(out=c_mask[:], in_=mask_h).then_inc(SM["s_wa"], 16)
            sync.dma_start(out=c_1r[:], in_=onesr_h).then_inc(SM["s_wa"], 16)
            sync.dma_start(out=c_1c[:], in_=onesc_h).then_inc(SM["s_wa"], 16)
            sync.dma_start(out=c_1rf[:], in_=onesrf_h).then_inc(SM["s_wa"], 16)
            for j in range(0, NPC, 2):
                psrc, _, w = piece_desc(j)
                dstv = stage[0][:, 0:w]
                if j >= 16:
                    dstv = dstv.rearrange("p (s f) -> p s f", s=4)
                if j >= 2:
                    sync.wait_ge(SM["c_abs"], j - 1)
                sync.dma_start(out=dstv, in_=psrc).then_inc(SM["s_stg0"], 16)
            sync.wait_ge(SM["c_abs"], NPC)
            for j in range(NPC):
                psrc, _, w = piece_desc(j)
                dstv = stage[j % 2][:, 0:w]
                if j >= 16:
                    dstv = dstv.rearrange("p (s f) -> p s f", s=4)
                if j >= 2:
                    sync.wait_ge(SM["v_stg"], j - 1)
                sync.dma_start(out=dstv, in_=psrc).then_inc(SM["s_st2a" if j % 2 == 0 else "s_st2b"], 16)
            for i in range(nt):
                if i >= 2:
                    sync.wait_ge(SM["v_pe"], 8 * (i - 1))
                src = x_h[i * TOK:(i + 1) * TOK, :].rearrange("(s p) d -> p s d", p=128)
                dst = x_sb[i % 2][:].rearrange("p (s d) -> p s d", s=NSUB)
                sync.dma_start(out=dst, in_=src).then_inc(SM[f"s_x{i % 2}"], 16)
            sync.wait_ge(SM["s_qo0"], 16 * (NPC // 2))
            sync.wait_ge(SM["s_qo1"], 16 * (NPC // 2))
            sync.wait_ge(SM["v_pe"], 8 * nt)
            sync.wait_ge(SM["c_AVev"], 16 * nt)
            sync.wait_ge(SM["s_xno0"], 16 * ((nt + 1) // 2))
            sync.wait_ge(SM["s_xno1"], 16 * (nt // 2))
            sync.dma_start(out=wgq[:].rearrange("p (m f) -> p m f", m=8),
                           in_=wgq_h.rearrange("(m p) f -> p m f", p=128)).then_inc(SM["s_wb1"], 16)
            sync.dma_start(out=wuq[:].rearrange("p (m f) -> p m f", m=8),
                           in_=wuq_h.rearrange("(m p) f -> p m f", p=128)).then_inc(SM["s_wb1"], 16)
            for i in range(nt):
                for s in range(NSUB):
                    k = i * NSUB + s
                    if k >= 2:
                        sync.wait_ge(SM["v_xq"], k - 1)
                    sync.wait_ge(SM[f"s_xno{i % 2}"], 16 * (i // 2 + 1))
                    src = xnew_h[i * TOK + s * 128: i * TOK + (s + 1) * 128, :]
                    sync.dma_start(out=xnsub[k % 2][:], in_=src).then_inc(SM[f"s_xni{k % 2}"], 16)
            sync.wait_ge(SM["s_hqo0"], 16 * (nt * NSUB // 2))
            sync.wait_ge(SM["s_hqo1"], 16 * (nt * NSUB // 2))
            sync.dma_start(out=wdq[:].rearrange("p (m f) -> p m f", m=32),
                           in_=wdq_h.rearrange("(m p) f -> p m f", p=128)).then_inc(SM["s_wb2"], 16)
            for i in range(nt):
                for s in range(NSUB):
                    k = i * NSUB + s
                    if k >= 2:
                        sync.wait_ge(SM["v_o"], 2 * (k - 1))
                    src = xnew_h[i * TOK + s * 128: i * TOK + (s + 1) * 128, :]
                    sync.dma_start(out=xnsub2[k % 2][:], in_=src).then_inc(SM["s_xnia" if k % 2 == 0 else "s_xnib"], 16)
                    if k >= 2:
                        sync.wait_ge(SM["t_d"], 2 * (k - 1))
                    sync.wait_ge(SM[f"s_hqo{k % 2}"], 16 * (k // 2 + 1))
                    sync.dma_start(out=hq_in[k % 2][:], in_=hq_h[k, :, :]).then_inc(SM["s_hqia" if k % 2 == 0 else "s_hqib"], 16)

        def gpsimd_body(g):
            g.dma_start(out=wqkvT[:].rearrange("p (m f) -> p m f", m=8),
                        in_=wqkvT_h.rearrange("(m p) f -> p m f", p=128)).then_inc(SM["s_wag"], 16)
            g.dma_start(out=wprojT[:].rearrange("p (m f) -> p m f", m=8),
                        in_=wprojT_h.rearrange("(m p) f -> p m f", p=128)).then_inc(SM["s_wag"], 16)
            g.dma_start(out=bproj[:], in_=bproj_h).then_inc(SM["s_wag"], 16)
            for j in range(1, NPC, 2):
                psrc, _, w = piece_desc(j)
                dstv = stage[1][:, 0:w]
                if j >= 16:
                    dstv = dstv.rearrange("p (s f) -> p s f", s=4)
                if j >= 3:
                    g.wait_ge(SM["c_abs"], j - 1)
                g.dma_start(out=dstv, in_=psrc).then_inc(SM["s_sg1"], 16)


            for j in range(NPC):
                psrc, pdst, w = piece_desc(j)
                outv = qout[j % 2][:, 0:w]
                if j >= 16:
                    outv = outv.rearrange("p (s f) -> p s f", s=4)
                g.wait_ge(SM["v_q"], j + 1)
                g.dma_start(out=pdst, in_=outv).then_inc(SM[f"s_qo{j % 2}"], 16)
            for i in range(nt):
                g.wait_ge(SM["v_pe"], 8 * (i + 1))
                dst = xnew_h[i * TOK:(i + 1) * TOK, :].rearrange("(s p) d -> p s d", p=128)
                srcv = xnew_sb[i % 2][:].rearrange("p (s d) -> p s d", s=NSUB)
                g.dma_start(out=dst, in_=srcv).then_inc(SM[f"s_xno{i % 2}"], 16)
            for i in range(nt):
                for s in range(NSUB):
                    k = i * NSUB + s
                    g.wait_ge(SM["c_hqfm"], 8 * (k + 1))
                    g.dma_start(out=hq_h[k, :, :], in_=hq_fm[k % 2][:]).then_inc(SM[f"s_hqo{k % 2}"], 16)
            for i in range(nt):
                for s in range(NSUB):
                    k = i * NSUB + s
                    g.wait_ge(SM["v_o"], 2 * (k + 1))
                    dst = out_h[i * TOK + s * 128: i * TOK + (s + 1) * 128, :]
                    g.dma_start(out=dst, in_=out_sb[k % 2][:]).then_inc(SM[f"s_out{k % 2}"], 16)
            g.wait_ge(SM["s_out0"], 16 * (nt * NSUB // 2))
            g.wait_ge(SM["s_out1"], 16 * (nt * NSUB // 2))

        def tensor_body(t):
            t.wait_ge(SM["c_abs"], NPC)
            t.wait_ge(SM["s_wa"], 16 * 6)
            t.wait_ge(SM["s_wag"], 16 * 3)
            t.matmul(P[0][0:1, 0:NPC], c_1c[:], acc_sb[:], start=True, stop=True).then_inc(SM["t_sums"], 1)
            t.wait_ge(SM["v_scales"], 1)
            t.matmul(P[1][:, 0:3], c_1rf[0:1, :], sums_sb[0:1, 8:11], start=True, stop=True).then_inc(SM["g_bc"], 1)

            for i in range(nt):
                t.wait_ge(SM["c_xn"], i + 1)
                for b in range(8):
                    s2, q = b // 2, b % 2
                    if i >= 1 and b < 4:
                        if b % 4 == 0:
                            t.wait_ge(SM["v_pe"], 8 * i - 3)
                        elif b % 4 == 1:
                            t.wait_ge(SM["v_pe"], 8 * i - 2)
                        elif b % 4 == 2:
                            t.wait_ge(SM["c_Atev"], 16 * i - 2)
                        else:
                            t.wait_ge(SM["c_AVev"], 16 * i - 2)
                    if i * 8 + b >= 4:
                        t.wait_ge(SM["c_xnfm"], i * 8 + b - 3)
                    for kk in range(4):
                        mc = q * 4 + kk
                        ins = t.matmul(P[[2, 3, 0, 1][b % 4]][:, kk * 128:(kk + 1) * 128],
                                       xn_tm[:, s2 * 1024 + mc * 128: s2 * 1024 + (mc + 1) * 128],
                                       c_id[:], start=True, stop=True)
                    ins.then_inc(SM["t_xnT"], 1)
                t.wait_ge(SM["c_xnfm"], 8 * (i + 1))
                if i >= 1:
                    t.wait_ge(SM["v_ksw"], 8 * i)
                for c in range(24):
                    if i * 24 + c >= 2:
                        t.wait_ge(SM["c_qkvev"], i * 24 + c - 1)
                    for m in range(8):
                        ins = t.matmul(P[2 + c % 2][:],
                                       wqkvT[:, m * 3072 + c * 128: m * 3072 + (c + 1) * 128],
                                       xn_fm[:, m * 512:(m + 1) * 512],
                                       start=(m == 0), stop=(m == 7))
                    ins.then_inc(SM["t_qkv"], 1)
                t.wait_ge(SM["c_qkvev"], 24 * (i + 1))
                for pc in range(8):
                    if i * 8 + pc >= 2:
                        t.wait_ge(SM["v_ksw"], i * 8 + pc - 1)
                    t.matmul(P[2 + pc % 2][:], c_sw[:], kt[:, pc * 512:(pc + 1) * 512],
                             start=True, stop=True).then_inc(SM["t_ksw"], 1)

                def t_at(bb):
                    ib = i * NBAT + bb
                    BA = P[6] if (bb % 2 == 0 or bb == NBAT - 1) else P[0]
                    t.wait_ge(SM["v_A"], ib + 1)
                    if bb == NBAT - 1:
                        t.wait_ge(SM["c_Atev"], ib)
                    elif ib >= 2:
                        t.wait_ge(SM["c_Atev"], ib - 1)
                    for g4 in range(4):
                        ins = t.matmul(BA[:, g4 * 128:(g4 + 1) * 128],
                                       asb[bb % 4][:, g4 * 128:(g4 + 1) * 128], c_id[:],
                                       start=True, stop=True)
                    ins.then_inc(SM["t_At"], 1)

                def t_av(bb):
                    ib = i * NBAT + bb
                    BV = P[7] if (bb % 2 == 0 or bb == NBAT - 1) else P[1]
                    t.wait_ge(SM["c_Atev"], ib + 1)
                    t.wait_ge(SM["c_V8ev"], ib + 1)
                    if bb == NBAT - 4:
                        t.wait_ge(SM["c_V8ev"], i * NBAT + NBAT)
                    if bb == NBAT - 1:
                        t.wait_ge(SM["c_AVev"], ib)
                    elif ib >= 2:
                        t.wait_ge(SM["c_AVev"], ib - 1)
                    for g4 in range(4):
                        vs = slice(g4 * 64, (g4 + 1) * 64)
                        t.matmul(BV[0:64, 256 + g4 * 64: 256 + (g4 + 1) * 64],
                                 v8sb[bb % 4][:, vs], atsb[bb % 4][:, g4 * 128: g4 * 128 + 64],
                                 start=True, stop=True)
                        ins = t.matmul(BV[64:128, 256 + g4 * 64: 256 + (g4 + 1) * 64],
                                       v8sb[bb % 4][:, vs], atsb[bb % 4][:, g4 * 128 + 64: (g4 + 1) * 128],
                                       start=True, stop=True)
                    ins.then_inc(SM["t_AV"], 1)

                def t_proj(s, pc):
                    j = s * 2 + pc
                    B = P[4 + pc] if s == 3 else P[2 + j % 2]
                    if s == 3:
                        t.wait_ge(SM["v_S2"], i * NBAT + 15 + pc)
                    elif i * 8 + j >= 2:
                        t.wait_ge(SM["v_pe"], i * 8 + j - 1)
                    t.wait_ge(SM["c_AVev"], i * NBAT + 4 * (s + 1))
                    for ch in range(8):
                        t.matmul(B[:],
                                 attn_fm[:, ch * 512 + s * 128: ch * 512 + (s + 1) * 128],
                                 wprojT[:, ch * 1024 + pc * 512: ch * 1024 + (pc + 1) * 512],
                                 start=(ch == 0), stop=False)
                    t.matmul(B[:], c_1r[0:1, :],
                             bproj[0:1, pc * 512:(pc + 1) * 512],
                             start=False, stop=True).then_inc(SM["t_pj"], 1)

                for bb in range(NBAT):
                    ib = i * NBAT + bb
                    Sb = P[4 + bb % 2]
                    BV = P[7] if (bb % 2 == 0 or bb == NBAT - 1) else P[1]
                    t.wait_ge(SM["v_ksw"], i * 8 + bb // 2 + 1)
                    if ib >= 2:
                        t.wait_ge(SM["v_S2"], ib - 1)
                    if i >= 1 and bb <= 1:
                        t.wait_ge(SM["v_pe"], 8 * i - 1 + bb)
                    for g4 in range(4):
                        G = bb * 4 + g4
                        qs = slice(G * 64, (G + 1) * 64)
                        co = g4 * 128
                        t.matmul(Sb[0:64, co:co + 64], qt[0:64, qs], kt[0:64, qs], start=True, stop=True)
                        t.matmul(Sb[0:64, co + 64:co + 128], qt[0:64, qs], ktx[0:64, qs], start=True, stop=True)
                        t.matmul(Sb[64:128, co:co + 64], qt[64:128, qs], ktx[64:128, qs], start=True, stop=True)
                        ins = t.matmul(Sb[64:128, co + 64:co + 128], qt[64:128, qs], kt[64:128, qs], start=True, stop=True)
                    ins.then_inc(SM["t_S"], 1)
                    if bb == NBAT - 1:
                        t.wait_ge(SM["c_V8ev"], ib)
                    elif ib >= 2:
                        t.wait_ge(SM["c_V8ev"], ib - 1)
                    if bb <= 1:
                        if i >= 1:
                            t.wait_ge(SM["c_AVev"], 16 * i)
                    elif bb == NBAT - 1:
                        t.wait_ge(SM["c_AVev"], ib - 4)
                    elif ib >= 6:
                        t.wait_ge(SM["c_AVev"], ib - 5)
                    for g4 in range(4):
                        G = bb * 4 + g4
                        qs = slice(G * 64, (G + 1) * 64)
                        t.matmul(BV[0:64, g4 * 64:(g4 + 1) * 64], vt[0:64, qs],
                                 c_id[0:64, 0:64], start=True, stop=True)
                        ins = t.matmul(BV[64:128, g4 * 64:(g4 + 1) * 64], vt[64:128, qs],
                                       c_id[64:128, 64:128], start=True, stop=True)
                    ins.then_inc(SM["t_V8"], 1)
                    if bb >= 3:
                        t_at(bb - 3)
                    if bb >= 4:
                        t_av(bb - 4)
                    if bb == 9:
                        t_proj(0, 0)
                        t_proj(0, 1)
                    elif bb == 12:
                        t_proj(1, 0)
                        t_proj(1, 1)
                    elif bb == 15:
                        t_proj(2, 0)
                        t_proj(2, 1)
                t_at(NBAT - 3)
                t_av(NBAT - 4)
                t_at(NBAT - 2)
                t_av(NBAT - 3)
                t_at(NBAT - 1)
                t_av(NBAT - 2)
                t_av(NBAT - 1)
                t_proj(3, 0)
                t_proj(3, 1)

            t.wait_ge(SM["s_wb1"], 32)
            ntot_t = nt * NSUB

            def xqT(k):
                t.wait_ge(SM["v_xq"], k + 1)
                for b in range(2):
                    if k >= 2:
                        t.wait_ge(SM["c_hqfm"], 8 * (k - 1))
                    for kk in range(4):
                        j = b * 4 + kk
                        ins = t.matmul(P[6 + b][:, kk * 128:(kk + 1) * 128],
                                       xq_tm[k % 3][:, j * 128:(j + 1) * 128], c_id[:],
                                       start=True, stop=True)
                    ins.then_inc(SM["t_xqT"], 1)
            xqT(0)

            def hqT(k, b):
                t.wait_ge(SM["v_hq"], 2 * k + (1 if b < 4 else 2))
                if b < 2 and k + 1 < ntot_t:
                    t.wait_ge(SM["c_xqfm"], 2 * (k + 1) + 1 + b)
                elif k * 8 + b >= 2:
                    t.wait_ge(SM["c_hqfm"], k * 8 + b - 1)
                for kk in range(4):
                    j = b * 4 + kk
                    ins = t.matmul(P[6 + b % 2][:, kk * 128:(kk + 1) * 128],
                                   hq_tm[:, j * 128:(j + 1) * 128], c_id[:],
                                   start=True, stop=True)
                ins.then_inc(SM["t_hqT"], 1)

            for i in range(nt):
                for s in range(NSUB):
                    k = i * NSUB + s
                    t.wait_ge(SM["c_xqfm"], 2 * (k + 1))
                    for pc in range(8):
                        jg = k * 8 + pc
                        if jg >= 3:
                            t.wait_ge(SM["c_g"], jg - 2)
                        for m in range(8):
                            ins = t.matmul(P[[2, 3, 0][jg % 3]][:],
                                           xq_fm[k % 2][:, m * 128:(m + 1) * 128],
                                           wgq[:, m * FI + pc * 512: m * FI + (pc + 1) * 512],
                                           start=(m == 0), stop=(m == 7))
                        ins.then_inc(SM["t_g"], 1)
                        if jg >= 3:
                            t.wait_ge(SM["c_u"], jg - 2)
                        for m in range(8):
                            ins = t.matmul(P[[4, 5, 1][jg % 3]][:],
                                           xq_fm[k % 2][:, m * 128:(m + 1) * 128],
                                           wuq[:, m * FI + pc * 512: m * FI + (pc + 1) * 512],
                                           start=(m == 0), stop=(m == 7))
                        ins.then_inc(SM["t_u"], 1)
                        if k >= 1:
                            hqT(k - 1, pc)
                    if k + 1 < ntot_t:
                        xqT(k + 1)
            for b in range(8):
                hqT(ntot_t - 1, b)

            t.wait_ge(SM["s_wb2"], 16)
            for i in range(nt):
                for s in range(NSUB):
                    k = i * NSUB + s
                    t.wait_ge(SM["s_hqia" if k % 2 == 0 else "s_hqib"], 16 * (k // 2 + 1))
                    for pc in range(2):
                        j = k * 2 + pc
                        if j >= 4:
                            t.wait_ge(SM["v_o"], j - 3)
                        for m in range(32):
                            ins = t.matmul(P[j % 4][:],
                                           hq_in[k % 2][:, m * 128:(m + 1) * 128],
                                           wdq[:, m * 1024 + pc * 512: m * 1024 + (pc + 1) * 512],
                                           start=(m == 0), stop=(m == 31))
                        ins.then_inc(SM["t_d"], 1)

        def scalar_body(sc):
            for j in range(NPC):
                _, _, w = piece_desc(j)
                sc.wait_ge(SM["s_stg0" if j % 2 == 0 else "s_sg1"], 16 * (j // 2 + 1))
                if j > 0:
                    sc.drain()
                sc.activation(qout[0][:, 0:w], stage[j % 2][:, 0:w], AF.Abs,
                              accum_out=acc_sb[:, j:j + 1]).then_inc(SM["c_abs"], 1)

            def c_sqrt(i):
                sc.wait_ge(SM["v_st1"], i + 1)
                sc.activation(lns[:, 24:28], lns[:, 20:24], AF.Sqrt).then_inc(SM["c_sq"], 1)
            c_sqrt(0)
            for i in range(nt):

                for b in range(8):
                    s2, q = b // 2, b % 2
                    sc.wait_ge(SM["t_xnT"], i * 8 + b + 1)
                    dst = xn_fm[:].rearrange("p (mc t) -> p mc t", mc=8)[
                        :, q * 4:(q + 1) * 4, s2 * 128:(s2 + 1) * 128]
                    srcv = P[[2, 3, 0, 1][b % 4]][:].rearrange("p (mc4 t) -> p mc4 t", mc4=4)
                    sc.copy(dst, srcv).then_inc(SM["c_xnfm"], 1)
                for c in range(24):
                    sc.wait_ge(SM["t_qkv"], i * 24 + c + 1)
                    src = P[2 + c % 2][:].rearrange("p (G tb) -> p G tb", G=NGRP)
                    if c < 8:
                        dst = qt[:].rearrange("p (G c3 tb) -> p G c3 tb", G=NGRP, c3=8)[:, :, c, :]
                        ins = sc.mul(dst, src, 0.125)
                    elif c < 16:
                        dst = kt[:].rearrange("p (G c3 tb) -> p G c3 tb", G=NGRP, c3=8)[:, :, c - 8, :]
                        ins = sc.copy(dst, src)
                    else:
                        dst = vt[:].rearrange("p (G c3 tb) -> p G c3 tb", G=NGRP, c3=8)[:, :, c - 16, :]
                        ins = sc.copy(dst, src)
                    ins.then_inc(SM["c_qkvev"], 1)
                if i + 1 < nt:
                    c_sqrt(i + 1)

                def sc_atev(bb):
                    ib = i * NBAT + bb
                    BA = P[6] if (bb % 2 == 0 or bb == NBAT - 1) else P[0]
                    sc.wait_ge(SM["t_At"], ib + 1)
                    if ib >= 4:
                        sc.wait_ge(SM["t_AV"], ib - 3)
                    sc.copy(atsb[bb % 4][:], BA[:]).then_inc(SM["c_Atev"], 1)

                def sc_avev(bb):
                    ib = i * NBAT + bb
                    BV = P[7] if (bb % 2 == 0 or bb == NBAT - 1) else P[1]
                    sc.wait_ge(SM["t_AV"], ib + 1)
                    srcv = BV[:, 256:512].rearrange("p (g4 c8 tb) -> p g4 c8 tb", g4=4, c8=8)
                    dst = attn_fm[:].rearrange("p (c8 t) -> p c8 t", c8=8)[
                        :, :, bb * 32:(bb + 1) * 32].rearrange("p c8 (g4 tb) -> p g4 c8 tb", g4=4)
                    sc.copy(dst, srcv).then_inc(SM["c_AVev"], 1)

                for bb in range(NBAT):
                    ib = i * NBAT + bb
                    BV = P[7] if (bb % 2 == 0 or bb == NBAT - 1) else P[1]
                    mxc = 60 if bb % 2 == 0 else 48
                    if bb >= 3:
                        sc_atev(bb - 3)
                    acc = 64 if bb % 2 == 0 else 68
                    sc.wait_ge(SM["v_S2"], ib + 1)
                    for g4 in range(4):
                        ins = sc.activation(esb[bb % 2][:, g4 * 128:(g4 + 1) * 128],
                                            s1f[bb % 2][:, g4 * 128:(g4 + 1) * 128], AF.Exp,
                                            bias=lns[:, mxc + g4:mxc + g4 + 1],
                                            accum_out=lns[:, acc + g4:acc + g4 + 1])
                    ins.then_inc(SM["c_E"], 1)
                    sc.wait_ge(SM["t_V8"], ib + 1)
                    if ib >= 4:
                        sc.wait_ge(SM["t_AV"], ib - 3)
                    sc.copy(v8sb[bb % 4][:], BV[:, 0:256]).then_inc(SM["c_V8ev"], 1)
                    if bb >= 4:
                        sc_avev(bb - 4)
                sc_atev(NBAT - 3)
                sc_avev(NBAT - 4)
                sc_atev(NBAT - 2)
                sc_avev(NBAT - 3)
                sc_atev(NBAT - 1)
                sc_avev(NBAT - 2)
                sc_avev(NBAT - 1)

            ntot = nt * NSUB

            def sq2(kq):
                L = (kq % 2) * 48
                sc.wait_ge(SM["v_st2"], kq + 1)
                sc.activation(ln2[:, L + 3:L + 4], ln2[:, L + 2:L + 3], AF.Sqrt).then_inc(SM["c_sq2"], 1)
            sq2(0)
            if ntot > 1:
                sq2(1)

            def sc_hqev(kq, b):
                sc.wait_ge(SM["t_hqT"], kq * 8 + b + 1)
                if kq >= 2:
                    sc.wait_ge(SM[f"s_hqo{kq % 2}"], 16 * (kq // 2))
                sc.copy(hq_fm[kq % 2][:, b * 512:(b + 1) * 512],
                        P[6 + b % 2][:]).then_inc(SM["c_hqfm"], 1)

            def xq_evacs(k):
                if k >= 2:
                    sc.wait_ge(SM["t_u"], 8 * (k - 1))
                for b in range(2):
                    sc.wait_ge(SM["t_xqT"], k * 2 + b + 1)
                    sc.copy(xq_fm[k % 2][:, b * 512:(b + 1) * 512], P[6 + b][:]).then_inc(SM["c_xqfm"], 1)

            xq_evacs(0)
            for i in range(nt):
                for s in range(NSUB):
                    k = i * NSUB + s
                    if k + 2 < ntot:
                        sq2(k + 2)
                    for pc in range(8):
                        jg = k * 8 + pc
                        sc.wait_ge(SM["t_g"], jg + 1)
                        if jg >= 2:
                            sc.wait_ge(SM["v_M"], jg - 1)
                        sc.activation(silug[jg % 2][:], P[[2, 3, 0][jg % 3]][:], AF.Silu,
                                      scale=agm[:, k % 4:k % 4 + 1]).then_inc(SM["c_g"], 1)
                        sc.wait_ge(SM["t_u"], jg + 1)
                        sc.copy(u_sb[jg % 2][:], P[[4, 5, 1][jg % 3]][:]).then_inc(SM["c_u"], 1)
                        if k >= 1:
                            sc_hqev(k - 1, pc)
                    if k + 1 < ntot:
                        xq_evacs(k + 1)
            for b in range(8):
                sc_hqev(ntot - 1, b)

        def vector_body(v):
            TS = v.tensor_scalar
            TT = v.tensor_tensor
            STT = v.scalar_tensor_tensor

            v.wait_ge(SM["t_sums"], 1)
            v.tensor_copy(sums_sb[0:1, 0:NPC], P[0][0:1, 0:NPC])
            v.drain()
            v.reduce_sum(sums_sb[0:1, 48:49], sums_sb[0:1, 0:8], axis=AX.X)
            v.reduce_sum(sums_sb[0:1, 49:50], sums_sb[0:1, 8:16], axis=AX.X)
            v.reduce_sum(sums_sb[0:1, 50:51], sums_sb[0:1, 16:24], axis=AX.X)
            v.drain()
            TS(sums_sb[0:1, 51:53], sums_sb[0:1, 48:50], 1.0 / (D * FI), 1e-5, ALU.mult, ALU.max)
            TS(sums_sb[0:1, 53:54], sums_sb[0:1, 50:51], 1.0 / (FI * D), 1e-5, ALU.mult, ALU.max)
            v.drain()
            v.reciprocal(sums_sb[0:1, 8:11], sums_sb[0:1, 51:54]).then_inc(SM["v_scales"], 1)
            v.wait_ge(SM["g_bc"], 1)
            v.tensor_copy(wsc[:, 0:3], P[1][:, 0:3])
            v.drain()
            for j in range(NPC):
                _, _, w = piece_desc(j)
                v.wait_ge(SM["s_st2a" if j % 2 == 0 else "s_st2b"], 16 * (j // 2 + 1))
                if j >= 2:
                    v.wait_ge(SM[f"s_qo{j % 2}"], 16 * (j // 2))
                TS(qtmp[:, 0:w], stage[j % 2][:, 0:w], wsc[:, sw_col(j):sw_col(j) + 1],
                   MAGIC, ALU.mult, ALU.add).then_inc(SM["v_stg"], 1)
                v.drain()
                TS(qtmp[:, 0:w], qtmp[:, 0:w], MAGIC, None, ALU.subtract)
                v.drain()
                TS(qout[j % 2][:, 0:w], qtmp[:, 0:w], -1.0, 1.0, ALU.max, ALU.min).then_inc(SM["v_q"], 1)
                v.drain()

            def v_ln(i):
                xb = x_sb[i % 2]
                ro = 36 if i % 2 == 0 else 44
                v.wait_ge(SM[f"s_x{i % 2}"], 16 * (i // 2 + 1))
                for s2 in range(NSUB):
                    xs = xb[:, s2 * 1024:(s2 + 1) * 1024]
                    if s2 > 0:
                        v.drain()
                    v.bn_stats(lns[:, 0:6], xs[:, 0:512])
                    v.bn_stats(lns[:, 6:12], xs[:, 512:1024])
                    v.drain()
                    v.bn_aggr(lns[:, 12 + 2 * s2:14 + 2 * s2], lns[:, 0:12])
                v.drain()
                pairs = lns[:, 12:20].rearrange("p (s k) -> p s k", k=2)
                v.tensor_copy(lns[:, 52:56].rearrange("p s -> p s ()"), pairs[:, :, 0:1])
                v.tensor_copy(lns[:, 56:60].rearrange("p s -> p s ()"), pairs[:, :, 1:2])
                v.drain()
                TS(lns[:, 20:24], lns[:, 56:60], EPS, None, ALU.add).then_inc(SM["v_st1"], 1)
                v.wait_ge(SM["c_sq"], i + 1)
                v.reciprocal(lns[:, 28:32], lns[:, 24:28])
                v.drain()
                TT(lns[:, 32:36], lns[:, 28:32], lns[:, 28:32], ALU.mult)
                v.drain()
                TT(lns[:, 60:64], lns[:, 32:36], lns[:, 20:24], ALU.mult)
                v.drain()
                TS(lns[:, 32:36], lns[:, 60:64], -0.5, 1.5, ALU.mult, ALU.add)
                v.drain()
                TT(lns[:, ro:ro + 4], lns[:, 28:32], lns[:, 32:36], ALU.mult)
                v.drain()
                STT(lns[:, ro + 4:ro + 8], lns[:, 52:56], -1.0, lns[:, ro:ro + 4], ALU.mult, ALU.mult)
                v.drain()
                if i >= 1:
                    v.wait_ge(SM["t_xnT"], 8 * i)
                for s2 in range(NSUB):
                    ins = TS(xn_tm[:, s2 * 1024:(s2 + 1) * 1024],
                             xb[:, s2 * 1024:(s2 + 1) * 1024],
                             lns[:, ro + s2:ro + 1 + s2],
                             lns[:, ro + 4 + s2:ro + 5 + s2], ALU.mult, ALU.add)
                    if s2 == NSUB - 1:
                        ins.then_inc(SM["c_xn"], 1)

            v_ln(0)
            for i in range(nt):
                xb = x_sb[i % 2]
                if i + 1 < nt:
                    v_ln(i + 1)
                for pc in range(8):
                    v.wait_ge(SM["t_ksw"], i * 8 + pc + 1)
                    v.tensor_copy(ktx[:, pc * 512:(pc + 1) * 512], P[2 + pc % 2][:]).then_inc(SM["v_ksw"], 1)
                def v_s1(bb):
                    ib = i * NBAT + bb
                    Sb = P[4 + bb % 2]
                    mx = slice(60, 64) if bb % 2 == 0 else slice(48, 52)
                    v.wait_ge(SM["t_S"], ib + 1)
                    if ib >= 2:
                        v.wait_ge(SM["c_E"], ib - 1)
                    TT(s1f[bb % 2][:], Sb[:], c_mask[:], ALU.add)
                    v.drain()
                    v.reduce_max(lns[:, mx], s1f[bb % 2][:].rearrange("p (g c) -> p g c", g=4),
                                 axis=AX.X, negate=True).then_inc(SM["v_S2"], 1)

                def v_s2(bb):
                    ib = i * NBAT + bb
                    sm = slice(56, 60) if bb % 2 == 0 else slice(52, 56)
                    v.wait_ge(SM["c_E"], ib + 1)
                    acc2 = slice(64, 68) if bb % 2 == 0 else slice(68, 72)
                    v.reciprocal(lns[:, sm], lns[:, acc2])
                    v.drain()
                    if ib >= 4:
                        v.wait_ge(SM["t_At"], ib - 3)
                    TT(asb[bb % 4][:].rearrange("p (g c) -> p g c", g=4),
                       esb[bb % 2][:].rearrange("p (g c) -> p g c", g=4),
                       lns[:, sm].rearrange("p g -> p g ()").broadcast_to([128, 4, 128]),
                       ALU.mult).then_inc(SM["v_A"], 1)

                def v_pevac(s, pc):
                    j = s * 2 + pc
                    v.wait_ge(SM["t_pj"], i * 8 + j + 1)
                    if i == 1 and j == 0:
                        v.wait_ge(SM["s_qo0"], 16 * (NPC // 2))
                        v.wait_ge(SM["s_qo1"], 16 * (NPC // 2))
                    if i >= 2 and j == 0:
                        v.wait_ge(SM[f"s_xno{i % 2}"], 16 * (i // 2))
                    STT(xnew_sb[i % 2][:, s * 1024 + pc * 512: s * 1024 + (pc + 1) * 512],
                        (P[4 + pc] if s == 3 else P[2 + j % 2])[:], 1.0,
                        xb[:, s * 1024 + pc * 512: s * 1024 + (pc + 1) * 512],
                        ALU.mult, ALU.add).then_inc(SM["v_pe"], 1)

                for bb in range(NBAT):
                    v_s1(bb)
                    if bb >= 1:
                        v_s2(bb - 1)
                    if bb == 10:
                        v_pevac(0, 0)
                        v_pevac(0, 1)
                    elif bb == 13:
                        v_pevac(1, 0)
                        v_pevac(1, 1)
                v_s2(NBAT - 1)
                v_pevac(2, 0)
                v_pevac(2, 1)
                v_pevac(3, 0)
                v_pevac(3, 1)

            ntot = nt * NSUB

            def b1_stats(k):
                L = (k % 2) * 48
                xs = xnsub[k % 2]
                v.wait_ge(SM[f"s_xni{k % 2}"], 16 * (k // 2 + 1))
                if k >= 2:
                    v.wait_ge(SM["c_sq2"], k - 1)
                v.drain()
                v.bn_stats(ln2[:, L + 20:L + 26], xs[:, 0:512])
                v.bn_stats(ln2[:, L + 26:L + 32], xs[:, 512:1024])
                v.drain()
                v.bn_aggr(ln2[:, L + 0:L + 2], ln2[:, L + 20:L + 32])
                v.drain()
                TS(ln2[:, L + 2:L + 3], ln2[:, L + 1:L + 2], EPS, None, ALU.add)
                v.reduce_max(ln2[:, L + 7:L + 8], xs[:], axis=AX.X)
                v.tensor_reduce(ln2[:, L + 8:L + 9], xs[:], AX.X, ALU.min).then_inc(SM["v_st2"], 1)

            def b1_rest(k):
                L = (k % 2) * 48
                xs = xnsub[k % 2]
                v.wait_ge(SM["c_sq2"], k + 1)
                v.reciprocal(ln2[:, L + 4:L + 5], ln2[:, L + 3:L + 4])
                v.drain()
                TT(ln2[:, L + 5:L + 6], ln2[:, L + 4:L + 5], ln2[:, L + 4:L + 5], ALU.mult)
                v.drain()
                TT(ln2[:, L + 18:L + 19], ln2[:, L + 5:L + 6], ln2[:, L + 2:L + 3], ALU.mult)
                v.drain()
                TS(ln2[:, L + 19:L + 20], ln2[:, L + 18:L + 19], -0.5, 1.5, ALU.mult, ALU.add)
                v.drain()
                TT(ln2[:, L + 6:L + 7], ln2[:, L + 4:L + 5], ln2[:, L + 19:L + 20], ALU.mult)
                v.drain()
                TT(ln2[:, L + 9:L + 10], ln2[:, L + 7:L + 8], ln2[:, L + 0:L + 1], ALU.subtract)
                TT(ln2[:, L + 10:L + 11], ln2[:, L + 0:L + 1], ln2[:, L + 8:L + 9], ALU.subtract)
                v.drain()
                TT(ln2[:, L + 11:L + 12], ln2[:, L + 9:L + 10], ln2[:, L + 10:L + 11], ALU.max)
                v.drain()
                TT(ln2[:, L + 12:L + 13], ln2[:, L + 11:L + 12], ln2[:, L + 6:L + 7], ALU.mult)
                v.drain()
                TS(ln2[:, L + 13:L + 14], ln2[:, L + 12:L + 13], 1e-5, None, ALU.max)
                v.drain()
                v.reciprocal(ln2[:, L + 14:L + 15], ln2[:, L + 13:L + 14])
                v.drain()
                TS(ln2[:, L + 15:L + 16], ln2[:, L + 14:L + 15], 127.0, None, ALU.mult)
                v.drain()
                TT(ln2[:, L + 16:L + 17], ln2[:, L + 6:L + 7], ln2[:, L + 15:L + 16], ALU.mult)
                v.drain()
                STT(ln2[:, L + 17:L + 18], ln2[:, L + 0:L + 1], -1.0, ln2[:, L + 16:L + 17], ALU.mult, ALU.mult)
                v.drain()
                TS(ln2[:, L + 10:L + 11], ln2[:, L + 17:L + 18], 1.0, MAGIC, ALU.mult, ALU.add)
                if k >= 4:
                    v.wait_ge(SM["c_g"], 8 * (k - 3))
                TT(ln2[:, L + 18:L + 19], ln2[:, L + 15:L + 16], wsc[:, 0:1], ALU.mult)
                TT(ln2[:, L + 19:L + 20], ln2[:, L + 15:L + 16], wsc[:, 1:2], ALU.mult)
                v.drain()
                v.reciprocal(agm[:, k % 4:k % 4 + 1], ln2[:, L + 18:L + 19])
                v.reciprocal(agm[:, 4 + k % 4:5 + k % 4], ln2[:, L + 19:L + 20])
                if k >= 3:
                    v.wait_ge(SM["t_xqT"], 2 * (k - 2))
                TS(xqm[:], xs[:], ln2[:, L + 16:L + 17], ln2[:, L + 10:L + 11], ALU.mult, ALU.add)
                v.drain()
                TS(xq_tm[k % 3][:], xqm[:], MAGIC, None, ALU.subtract).then_inc(SM["v_xq"], 1)

            def b1_m(k):
                L = (k % 2) * 48
                for pc in range(8):
                    jg = k * 8 + pc
                    v.wait_ge(SM["c_u"], jg + 1)
                    v.wait_ge(SM["c_g"], jg + 1)
                    STT(m_sb[:, pc * 512:(pc + 1) * 512], u_sb[jg % 2][:],
                        agm[:, 4 + k % 4:5 + k % 4], silug[jg % 2][:],
                        ALU.mult, ALU.mult).then_inc(SM["v_M"], 1)
                    v.drain()
                    v.reduce_max(ln2[:, L + 32 + pc:L + 33 + pc],
                                 m_sb[:, pc * 512:(pc + 1) * 512],
                                 axis=AX.X, apply_absolute_value=True)
                v.drain()
                v.reduce_max(ln2[:, L + 40:L + 41], ln2[:, L + 32:L + 40], axis=AX.X)
                v.drain()
                TS(ln2[:, L + 41:L + 42], ln2[:, L + 40:L + 41], 1e-5, None, ALU.max)
                v.drain()
                v.reciprocal(ln2[:, L + 42:L + 43], ln2[:, L + 41:L + 42])
                v.drain()
                TS(ln2[:, L + 43:L + 44], ln2[:, L + 42:L + 43], 127.0, None, ALU.mult)
                v.drain()
                TT(ln2[:, L + 44:L + 45], ln2[:, L + 43:L + 44], wsc[:, 2:3], ALU.mult)
                v.drain()
                v.reciprocal(gam[:, k:k + 1], ln2[:, L + 44:L + 45])
                if k >= 1:
                    v.wait_ge(SM["t_hqT"], 8 * k)
                for hh in range(2):
                    hs = slice(hh * 2048, (hh + 1) * 2048)
                    TS(m_sb[:, hs], m_sb[:, hs], ln2[:, L + 43:L + 44], MAGIC, ALU.mult, ALU.add)
                    v.drain()
                    TS(hq_tm[:, hs], m_sb[:, hs], MAGIC, None, ALU.subtract).then_inc(SM["v_hq"], 1)
                    v.drain()

            b1_stats(0)
            b1_rest(0)
            if ntot > 1:
                b1_stats(1)
                b1_rest(1)
            for k in range(ntot):
                if k + 2 < ntot:
                    b1_stats(k + 2)
                    b1_rest(k + 2)
                b1_m(k)

            for i in range(nt):
                for s in range(NSUB):
                    k = i * NSUB + s
                    v.wait_ge(SM["s_xnia" if k % 2 == 0 else "s_xnib"], 16 * (k // 2 + 1))
                    for pc in range(2):
                        j = k * 2 + pc
                        v.wait_ge(SM["t_d"], j + 1)
                        if k >= 2 and pc == 0:
                            v.wait_ge(SM[f"s_out{k % 2}"], 16 * (k // 2))
                        STT(out_sb[k % 2][:, pc * 512:(pc + 1) * 512],
                            P[j % 4][:], gam[:, k:k + 1],
                            xnsub2[k % 2][:, pc * 512:(pc + 1) * 512],
                            ALU.mult, ALU.add).then_inc(SM["v_o"], 1)

        block.sync(sync_body)
        block.gpsimd(gpsimd_body)
        block.tensor(tensor_body)
        block.scalar(scalar_body)
        block.vector(vector_body)

    return nc


def _host_layouts(Wqkv, Wproj, bproj, w_gate, w_up, w_down):
    H, DH = 16, 64
    perm = np.empty(3 * D, dtype=np.int64)
    idx = 0
    for blk, off in ((0, 0), (1, 64), (2, 128)):
        for c in range(8):
            for par in range(2):
                h = 2 * c + par
                base = h * 192 + off
                perm[idx:idx + DH] = np.arange(base, base + DH)
                idx += DH
    wqkvT = np.ascontiguousarray(Wqkv[perm].T)
    hd_perm = np.empty(D, dtype=np.int64)
    idx = 0
    for c in range(8):
        for par in range(2):
            h = 2 * c + par
            hd_perm[idx:idx + DH] = np.arange(h * DH, (h + 1) * DH)
            idx += DH
    wprojT = np.ascontiguousarray(Wproj.T[hd_perm])
    bproj2 = np.ascontiguousarray(bproj.reshape(1, D))
    wgT = np.ascontiguousarray(w_gate.T)
    wuT = np.ascontiguousarray(w_up.T)
    wdT = np.ascontiguousarray(w_down.T)
    return wqkvT, wprojT, bproj2, wgT, wuT, wdT


def kernel(L0_attn_x, L0_attn_Wqkv, L0_attn_Wproj, L0_attn_bproj,
           L0_ffn_w_gate, L0_ffn_w_up, L0_ffn_w_down):
    x = np.ascontiguousarray(np.asarray(L0_attn_x, dtype=np.float32))
    B, N, _ = x.shape
    xf = x.reshape(B * N, D)
    wqkvT, wprojT, bproj2, wgT, wuT, wdT = _host_layouts(
        np.asarray(L0_attn_Wqkv, np.float32), np.asarray(L0_attn_Wproj, np.float32),
        np.asarray(L0_attn_bproj, np.float32), np.asarray(L0_ffn_w_gate, np.float32),
        np.asarray(L0_ffn_w_up, np.float32), np.asarray(L0_ffn_w_down, np.float32))
    per_core = (B * N) // N_CORES
    nt = per_core // TOK
    nc = build_nc(nt)
    in_maps = []
    for c in range(N_CORES):
        in_maps.append({
            "x": np.ascontiguousarray(xf[c * per_core:(c + 1) * per_core]),
            "wqkvT": wqkvT, "wprojT": wprojT, "bproj": bproj2,
            "wgT": wgT, "wuT": wuT, "wdT": wdT,
        })
    res = run_bass_kernel_spmd(nc, in_maps, core_ids=list(range(N_CORES)))
    out = np.concatenate([res.results[c]["out"] for c in range(N_CORES)], axis=0)
    return out.reshape(B, N, D).astype(np.float32)


# version 33
# speedup vs baseline: 1.0119x; 1.0119x over previous
import numpy as np
import ml_dtypes
import concourse.bass as bass
import concourse.mybir as mybir
from concourse.bass_utils import run_bass_kernel_spmd
from contextlib import ExitStack

F32 = mybir.dt.float32
BF16 = mybir.dt.bfloat16
ALU = mybir.AluOpType
AF = mybir.ActivationFunctionType
AX = mybir.AxisListType

MAGIC = float(np.float32(1.5 * 2.0 ** 23))
NEG = -30080.0
EPS = 1e-5
D = 1024
FI = 4096
TOK = 512
NSUB = 4
NGRP = TOK // 8
NBAT = NGRP // 4
NPC = 24
N_CORES = 8


def _consts():
    ident = np.eye(128, dtype=ml_dtypes.bfloat16)
    swap = np.zeros((128, 128), dtype=ml_dtypes.bfloat16)
    for i in range(128):
        swap[(i + 64) % 128, i] = 1.0
    mask = np.full((128, 512), NEG, dtype=ml_dtypes.bfloat16)
    r = np.arange(128) % 8
    c = np.arange(512) % 8
    mask[r[:, None] == c[None, :]] = 0.0
    ones_row = np.ones((1, 128), dtype=ml_dtypes.bfloat16)
    ones_col = np.ones((128, 1), dtype=np.float32)
    return ident, swap, mask, ones_row, ones_col


def build_nc(nt):
    n_tok = nt * TOK
    nc = bass.Bass("TRN2", target_bir_lowering=False)

    x_h = nc.dram_tensor("x", [n_tok, D], F32, kind="ExternalInput").ap()
    wqkvT_h = nc.dram_tensor("wqkvT", [D, 3 * D], F32, kind="ExternalInput").ap()
    wprojT_h = nc.dram_tensor("wprojT", [D, D], F32, kind="ExternalInput").ap()
    bproj_h = nc.dram_tensor("bproj", [1, D], F32, kind="ExternalInput").ap()
    wgT_h = nc.dram_tensor("wgT", [D, FI], F32, kind="ExternalInput").ap()
    wuT_h = nc.dram_tensor("wuT", [D, FI], F32, kind="ExternalInput").ap()
    wdT_h = nc.dram_tensor("wdT", [FI, D], F32, kind="ExternalInput").ap()
    out_h = nc.dram_tensor("out", [n_tok, D], F32, kind="ExternalOutput").ap()

    xnew_h = nc.dram_tensor("xnew_scr", [n_tok, D], F32).ap()
    hq_h = nc.dram_tensor("hq_scr", [nt * NSUB, 128, FI], BF16).ap()
    wgq_h = nc.dram_tensor("wgq_scr", [D, FI], BF16).ap()
    wuq_h = nc.dram_tensor("wuq_scr", [D, FI], BF16).ap()
    wdq_h = nc.dram_tensor("wdq_scr", [FI, D], BF16).ap()

    ident_np, swap_np, mask_np, ones_row_np, ones_col_np = _consts()
    ident_h = nc.inline_tensor(ident_np, name="c_ident_h").ap()
    swap_h = nc.inline_tensor(swap_np, name="c_swap_h").ap()
    mask_h = nc.inline_tensor(mask_np, name="c_mask_h").ap()
    onesr_h = nc.inline_tensor(ones_row_np, name="c_onesr_h").ap()
    onesc_h = nc.inline_tensor(ones_col_np, name="c_onesc_h").ap()
    onesrf_h = nc.inline_tensor(np.ones((1, 128), dtype=np.float32), name="c_onesrf_h").ap()

    def piece_desc(j):
        if j < 8:
            return wgT_h[j * 128:(j + 1) * 128, :], wgq_h[j * 128:(j + 1) * 128, :], FI
        if j < 16:
            k = j - 8
            return wuT_h[k * 128:(k + 1) * 128, :], wuq_h[k * 128:(k + 1) * 128, :], FI
        k = j - 16
        src = wdT_h[k * 512:(k + 1) * 512, :].rearrange("(s p) f -> p s f", p=128)
        dst = wdq_h[k * 512:(k + 1) * 512, :].rearrange("(s p) f -> p s f", p=128)
        return src, dst, FI

    def sw_col(j):
        return 0 if j < 8 else (1 if j < 16 else 2)

    with ExitStack() as ctx:
        e = ctx.enter_context
        c_id = e(nc.sbuf_tensor("c_id", [128, 128], BF16))
        c_sw = e(nc.sbuf_tensor("c_sw", [128, 128], BF16))
        c_1r = e(nc.sbuf_tensor("c_1r", [1, 128], BF16))
        c_1c = e(nc.sbuf_tensor("c_1c", [128, 1], F32))
        c_1rf = e(nc.sbuf_tensor("c_1rf", [1, 128], F32))
        wsc = e(nc.sbuf_tensor("wsc", [128, 4], F32))
        sums_sb = e(nc.sbuf_tensor("sums_sb", [1, 64], F32))
        acc_sb = e(nc.sbuf_tensor("acc_sb", [128, NPC], F32))
        gam = e(nc.sbuf_tensor("gam", [128, 64], F32))
        agm = e(nc.sbuf_tensor("agm", [128, 8], F32))
        lns = e(nc.sbuf_tensor("lns", [128, 72], F32))
        ln2 = e(nc.sbuf_tensor("ln2", [128, 96], F32))

        P = [e(nc.psum_tensor(f"pb{i}", [128, 512], F32)) for i in range(8)]

        sem_names = [
            "s_wa", "s_wag", "s_stg0", "s_stg1", "s_st2a", "s_st2b", "s_qo0", "s_qo1", "s_x0", "s_x1", "s_wb1", "s_wb2",
            "s_xno0", "s_xno1", "s_xni0", "s_xni1", "s_xni2", "s_hqo0", "s_hqo1", "s_hqia", "s_hqib", "s_xnia", "s_xnib",
            "s_out0", "s_out1", "g_bc",
            "c_abs", "t_sums", "v_scales", "v_stg", "v_q",
            "v_st1", "c_sq", "v_ln", "c_xn", "t_xnT", "c_xnfm",
            "t_qkv", "c_qkvev", "t_ksw", "v_ksw",
            "t_S", "v_S2", "c_E", "v_A",
            "t_At", "c_Atev", "t_V8", "c_V8ev", "t_AV", "c_AVev",
            "t_pj", "v_pe",
            "v_st2", "c_sq2", "v_xq", "t_xqT", "c_xqfm",
            "t_g", "c_g", "t_u", "c_u", "v_M", "v_hq", "t_hqT", "c_hqfm",
            "t_d", "v_o",
        ]
        SM = {n: e(nc.semaphore(n)) for n in sem_names}
        DU = {}

        def dmax(eng, out, in_, semname):
            ins = eng.dma_start(out=out, in_=in_).then_inc(SM[semname], 16)
            tot = 0
            for u in ins.ins.sync.on_update:
                if u.id == SM[semname].num:
                    tot += u.update_value
            if semname in DU:
                assert DU[semname] == tot, (semname, DU[semname], tot)
            else:
                DU[semname] = tot
            return ins

        def du(semname):
            return DU[semname]

        s1f0 = e(nc.sbuf_tensor("s1f0", [128, 512], F32))
        esb0 = e(nc.sbuf_tensor("esb0", [128, 512], F32))
        asb0 = e(nc.sbuf_tensor("asb0", [128, 512], BF16))
        atsb0 = e(nc.sbuf_tensor("atsb0", [128, 512], BF16))
        v8sb0 = e(nc.sbuf_tensor("v8sb0", [128, 256], BF16))

        AB = (nc.sbuf_base + 31) // 32 * 32
        KB = 1024

        def at(name, shape, dtype, off):
            assert AB + off + _bytes(shape, dtype) <= 229344, (name, off)
            return nc.alloc_sbuf_tensor_at(name, shape, dtype, offset=AB + off)

        def _bytes(shape, dtype):
            esz = 4 if dtype == F32 else 2
            return shape[1] * esz

        wqkvT = at("wqkvT_sb", [128, 8 * 3072], BF16, 0)
        wprojT = at("wprojT_sb", [128, 8 * 1024], BF16, 48 * KB)
        x_sb = [at(f"x_sb{i}", [128, 4096], F32, 64 * KB + 16 * KB * i) for i in range(2)]
        xn_tm = at("xn_tm", [128, 4096], BF16, 96 * KB)
        xn_fm = at("xn_fm", [128, 4096], BF16, 104 * KB)
        attn_fm = at("attn_fm", [128, 4096], BF16, 112 * KB)
        qt = at("qt", [128, 4096], BF16, 120 * KB)
        kt = at("kt", [128, 4096], BF16, 128 * KB)
        vt = at("vt", [128, 4096], BF16, 136 * KB)
        ktx = at("ktx", [128, 4096], BF16, 144 * KB)
        xnew_sb = [at(f"xnew{i}", [128, 4096], F32, 152 * KB + 16 * KB * i) for i in range(2)]
        c_mask = at("c_mask", [128, 512], BF16, 190 * KB + 512)
        bproj = at("bproj_sb", [1, 1024], BF16, 196 * KB + 512)
        s1f1 = at("s1f1", [128, 512], F32, 184 * KB)
        esb1 = at("esb1", [128, 512], F32, 186 * KB)
        asb1 = at("asb1", [128, 512], BF16, 188 * KB)
        atsb1 = at("atsb1", [128, 512], BF16, 189 * KB)
        v8sb1 = at("v8sb1", [128, 256], BF16, 190 * KB)
        asb2 = at("asb2", [128, 512], BF16, 191 * KB + 512)
        asb3 = at("asb3", [128, 512], BF16, 192 * KB + 512)
        atsb2 = at("atsb2", [128, 512], BF16, 193 * KB + 512)
        atsb3 = at("atsb3", [128, 512], BF16, 194 * KB + 512)
        v8sb2 = at("v8sb2", [128, 256], BF16, 195 * KB + 512)
        v8sb3 = at("v8sb3", [128, 256], BF16, 196 * KB)
        s1f = [s1f0, s1f1]
        esb = [esb0, esb1]
        asb = [asb0, asb1, asb2, asb3]
        atsb = [atsb0, atsb1, atsb2, atsb3]
        v8sb = [v8sb0, v8sb1, v8sb2, v8sb3]
        stage = [at(f"stage{i}", [128, FI], F32, 120 * KB + 16 * KB * i) for i in range(2)]
        qtmp = at("qtmp", [128, FI], F32, 152 * KB)
        qout = [at(f"qout{i}", [128, FI], BF16, 168 * KB + 8 * KB * i) for i in range(2)]
        wgq = at("wgq_sb", [128, 8 * FI], BF16, 0)
        wuq = at("wuq_sb", [128, 8 * FI], BF16, 64 * KB)
        xnsub = [at(f"xnsub{i}", [128, 1024], F32, 128 * KB + 4 * KB * i) for i in range(2)]
        xqm = at("xqm", [128, 1024], F32, 136 * KB)
        xq_tm = [at(f"xq_tm{i}", [128, 1024], BF16, 140 * KB + 2 * KB * i) for i in range(3)]
        xq_fm = [at(f"xq_fm{i}", [128, 1024], BF16, 146 * KB + 2 * KB * i) for i in range(2)]
        silug = [at(f"silug{i}", [128, 512], F32, 150 * KB + 2 * KB * i) for i in range(2)]
        u_sb = [at(f"u_sb{i}", [128, 512], F32, 154 * KB + 2 * KB * i) for i in range(2)]
        m_sb = at("m_sb", [128, FI], F32, 158 * KB)
        hq_tm = at("hq_tm", [128, FI], BF16, 174 * KB)
        hq_fm = [at(f"hq_fm{i}", [128, FI], BF16, 182 * KB + 8 * KB * i) for i in range(2)]
        wdq = at("wdq_sb", [128, 32 * 1024], BF16, 0)
        hq_in = [at(f"hq_in{i}", [128, FI], BF16, 64 * KB + 8 * KB * i) for i in range(2)]
        xnsub2 = [at(f"xnsub2{i}", [128, 1024], F32, 80 * KB + 4 * KB * i) for i in range(2)]
        out_sb = [at(f"out_sb{i}", [128, 1024], F32, 88 * KB + 4 * KB * i) for i in range(2)]

        block = e(nc.Block())

        def sync_body(sync):
            sync.dma_start(out=c_id[:], in_=ident_h).then_inc(SM["s_wa"], 16)
            sync.dma_start(out=c_sw[:], in_=swap_h).then_inc(SM["s_wa"], 16)
            sync.dma_start(out=c_mask[:], in_=mask_h).then_inc(SM["s_wa"], 16)
            sync.dma_start(out=c_1r[:], in_=onesr_h).then_inc(SM["s_wa"], 16)
            sync.dma_start(out=c_1c[:], in_=onesc_h).then_inc(SM["s_wa"], 16)
            sync.dma_start(out=c_1rf[:], in_=onesrf_h).then_inc(SM["s_wa"], 16)
            for j in range(NPC):
                psrc, _, w = piece_desc(j)
                dstv = stage[j % 2][:, 0:w]
                if j >= 16:
                    dstv = dstv.rearrange("p (s f) -> p s f", s=4)
                if j >= 2:
                    sync.wait_ge(SM["c_abs"], j - 1)
                sync.dma_start(out=dstv, in_=psrc).then_inc(SM[f"s_stg{j % 2}"], 16)
            sync.wait_ge(SM["c_abs"], NPC)
            for j in range(NPC):
                psrc, _, w = piece_desc(j)
                dstv = stage[j % 2][:, 0:w]
                if j >= 16:
                    dstv = dstv.rearrange("p (s f) -> p s f", s=4)
                if j >= 2:
                    sync.wait_ge(SM["v_stg"], j - 1)
                sync.dma_start(out=dstv, in_=psrc).then_inc(SM["s_st2a" if j % 2 == 0 else "s_st2b"], 16)
            for i in range(nt):
                if i >= 2:
                    sync.wait_ge(SM["v_pe"], 8 * (i - 1))
                src = x_h[i * TOK:(i + 1) * TOK, :].rearrange("(s p) d -> p s d", p=128)
                dst = x_sb[i % 2][:].rearrange("p (s d) -> p s d", s=NSUB)
                sync.dma_start(out=dst, in_=src).then_inc(SM[f"s_x{i % 2}"], 16)
            sync.wait_ge(SM["s_qo0"], 16 * (NPC // 2))
            sync.wait_ge(SM["s_qo1"], 16 * (NPC // 2))
            sync.wait_ge(SM["v_pe"], 8 * nt)
            sync.wait_ge(SM["c_AVev"], 16 * nt)
            sync.wait_ge(SM["s_xno0"], 16 * ((nt + 1) // 2))
            sync.wait_ge(SM["s_xno1"], 16 * (nt // 2))
            sync.dma_start(out=wgq[:].rearrange("p (m f) -> p m f", m=8),
                           in_=wgq_h.rearrange("(m p) f -> p m f", p=128)).then_inc(SM["s_wb1"], 16)
            sync.dma_start(out=wuq[:].rearrange("p (m f) -> p m f", m=8),
                           in_=wuq_h.rearrange("(m p) f -> p m f", p=128)).then_inc(SM["s_wb1"], 16)
            for i in range(nt):
                for s in range(NSUB):
                    k = i * NSUB + s
                    if k >= 2:
                        sync.wait_ge(SM["v_xq"], k - 1)
                    sync.wait_ge(SM[f"s_xno{i % 2}"], 16 * (i // 2 + 1))
                    src = xnew_h[i * TOK + s * 128: i * TOK + (s + 1) * 128, :]
                    sync.dma_start(out=xnsub[k % 2][:], in_=src).then_inc(SM[f"s_xni{k % 2}"], 16)
            sync.wait_ge(SM["s_hqo0"], 16 * (nt * NSUB // 2))
            sync.wait_ge(SM["s_hqo1"], 16 * (nt * NSUB // 2))
            sync.dma_start(out=wdq[:].rearrange("p (m f) -> p m f", m=32),
                           in_=wdq_h.rearrange("(m p) f -> p m f", p=128)).then_inc(SM["s_wb2"], 16)
            for i in range(nt):
                for s in range(NSUB):
                    k = i * NSUB + s
                    if k >= 2:
                        sync.wait_ge(SM["v_o"], 2 * (k - 1))
                    src = xnew_h[i * TOK + s * 128: i * TOK + (s + 1) * 128, :]
                    sync.dma_start(out=xnsub2[k % 2][:], in_=src).then_inc(SM["s_xnia" if k % 2 == 0 else "s_xnib"], 16)
                    if k >= 2:
                        sync.wait_ge(SM["t_d"], 2 * (k - 1))
                    sync.wait_ge(SM[f"s_hqo{k % 2}"], 16 * (k // 2 + 1))
                    sync.dma_start(out=hq_in[k % 2][:], in_=hq_h[k, :, :]).then_inc(SM["s_hqia" if k % 2 == 0 else "s_hqib"], 16)

        def gpsimd_body(g):
            g.dma_start(out=wqkvT[:].rearrange("p (m f) -> p m f", m=8),
                        in_=wqkvT_h.rearrange("(m p) f -> p m f", p=128)).then_inc(SM["s_wag"], 16)
            g.dma_start(out=wprojT[:].rearrange("p (m f) -> p m f", m=8),
                        in_=wprojT_h.rearrange("(m p) f -> p m f", p=128)).then_inc(SM["s_wag"], 16)
            g.dma_start(out=bproj[:], in_=bproj_h).then_inc(SM["s_wag"], 16)


            for j in range(NPC):
                psrc, pdst, w = piece_desc(j)
                outv = qout[j % 2][:, 0:w]
                if j >= 16:
                    outv = outv.rearrange("p (s f) -> p s f", s=4)
                g.wait_ge(SM["v_q"], j + 1)
                g.dma_start(out=pdst, in_=outv).then_inc(SM[f"s_qo{j % 2}"], 16)
            for i in range(nt):
                g.wait_ge(SM["v_pe"], 8 * (i + 1))
                dst = xnew_h[i * TOK:(i + 1) * TOK, :].rearrange("(s p) d -> p s d", p=128)
                srcv = xnew_sb[i % 2][:].rearrange("p (s d) -> p s d", s=NSUB)
                g.dma_start(out=dst, in_=srcv).then_inc(SM[f"s_xno{i % 2}"], 16)
            for i in range(nt):
                for s in range(NSUB):
                    k = i * NSUB + s
                    g.wait_ge(SM["c_hqfm"], 8 * (k + 1))
                    g.dma_start(out=hq_h[k, :, :], in_=hq_fm[k % 2][:]).then_inc(SM[f"s_hqo{k % 2}"], 16)
            for i in range(nt):
                for s in range(NSUB):
                    k = i * NSUB + s
                    g.wait_ge(SM["v_o"], 2 * (k + 1))
                    dst = out_h[i * TOK + s * 128: i * TOK + (s + 1) * 128, :]
                    g.dma_start(out=dst, in_=out_sb[k % 2][:]).then_inc(SM[f"s_out{k % 2}"], 16)
            g.wait_ge(SM["s_out0"], 16 * (nt * NSUB // 2))
            g.wait_ge(SM["s_out1"], 16 * (nt * NSUB // 2))

        def tensor_body(t):
            t.wait_ge(SM["c_abs"], NPC)
            t.wait_ge(SM["s_wa"], 16 * 6)
            t.wait_ge(SM["s_wag"], 16 * 3)
            t.matmul(P[0][0:1, 0:NPC], c_1c[:], acc_sb[:], start=True, stop=True).then_inc(SM["t_sums"], 1)
            t.wait_ge(SM["v_scales"], 1)
            t.matmul(P[1][:, 0:3], c_1rf[0:1, :], sums_sb[0:1, 8:11], start=True, stop=True).then_inc(SM["g_bc"], 1)

            for i in range(nt):
                t.wait_ge(SM["c_xn"], i + 1)
                for b in range(8):
                    s2, q = b // 2, b % 2
                    if i >= 1 and b < 4:
                        if b % 4 == 0:
                            t.wait_ge(SM["v_pe"], 8 * i - 3)
                        elif b % 4 == 1:
                            t.wait_ge(SM["v_pe"], 8 * i - 2)
                        elif b % 4 == 2:
                            t.wait_ge(SM["c_Atev"], 16 * i - 2)
                        else:
                            t.wait_ge(SM["c_AVev"], 16 * i - 2)
                    if i * 8 + b >= 4:
                        t.wait_ge(SM["c_xnfm"], i * 8 + b - 3)
                    for kk in range(4):
                        mc = q * 4 + kk
                        ins = t.matmul(P[[2, 3, 0, 1][b % 4]][:, kk * 128:(kk + 1) * 128],
                                       xn_tm[:, s2 * 1024 + mc * 128: s2 * 1024 + (mc + 1) * 128],
                                       c_id[:], start=True, stop=True)
                    ins.then_inc(SM["t_xnT"], 1)
                t.wait_ge(SM["c_xnfm"], 8 * (i + 1))
                if i >= 1:
                    t.wait_ge(SM["v_ksw"], 8 * i)
                for c in range(24):
                    if i * 24 + c >= 2:
                        t.wait_ge(SM["c_qkvev"], i * 24 + c - 1)
                    for m in range(8):
                        ins = t.matmul(P[2 + c % 2][:],
                                       wqkvT[:, m * 3072 + c * 128: m * 3072 + (c + 1) * 128],
                                       xn_fm[:, m * 512:(m + 1) * 512],
                                       start=(m == 0), stop=(m == 7))
                    ins.then_inc(SM["t_qkv"], 1)
                t.wait_ge(SM["c_qkvev"], 24 * (i + 1))
                for pc in range(8):
                    if i * 8 + pc >= 2:
                        t.wait_ge(SM["v_ksw"], i * 8 + pc - 1)
                    t.matmul(P[2 + pc % 2][:], c_sw[:], kt[:, pc * 512:(pc + 1) * 512],
                             start=True, stop=True).then_inc(SM["t_ksw"], 1)

                def t_at(bb):
                    ib = i * NBAT + bb
                    BA = P[6] if (bb % 2 == 0 or bb == NBAT - 1) else P[0]
                    t.wait_ge(SM["v_A"], ib + 1)
                    if bb == 0 and i >= 1:
                        t.wait_ge(SM["c_Atev"], ib)
                    if bb == NBAT - 1:
                        t.wait_ge(SM["c_Atev"], ib)
                    elif ib >= 2:
                        t.wait_ge(SM["c_Atev"], ib - 1)
                    for g4 in range(4):
                        ins = t.matmul(BA[:, g4 * 128:(g4 + 1) * 128],
                                       asb[bb % 4][:, g4 * 128:(g4 + 1) * 128], c_id[:],
                                       start=True, stop=True)
                    ins.then_inc(SM["t_At"], 1)

                def t_av(bb):
                    ib = i * NBAT + bb
                    BV = P[7] if (bb % 2 == 0 or bb == NBAT - 1) else P[1]
                    t.wait_ge(SM["c_Atev"], ib + 1)
                    t.wait_ge(SM["c_V8ev"], ib + 1)
                    if bb == NBAT - 4:
                        t.wait_ge(SM["c_V8ev"], i * NBAT + NBAT)
                    if bb == 0 and i >= 1:
                        t.wait_ge(SM["c_AVev"], ib)
                    if bb == NBAT - 1:
                        t.wait_ge(SM["c_AVev"], ib)
                    elif ib >= 2:
                        t.wait_ge(SM["c_AVev"], ib - 1)
                    for g4 in range(4):
                        vs = slice(g4 * 64, (g4 + 1) * 64)
                        t.matmul(BV[0:64, 256 + g4 * 64: 256 + (g4 + 1) * 64],
                                 v8sb[bb % 4][:, vs], atsb[bb % 4][:, g4 * 128: g4 * 128 + 64],
                                 start=True, stop=True)
                        ins = t.matmul(BV[64:128, 256 + g4 * 64: 256 + (g4 + 1) * 64],
                                       v8sb[bb % 4][:, vs], atsb[bb % 4][:, g4 * 128 + 64: (g4 + 1) * 128],
                                       start=True, stop=True)
                    ins.then_inc(SM["t_AV"], 1)

                def t_proj(s, pc):
                    j = s * 2 + pc
                    B = P[4 + pc] if s == 3 else P[2 + j % 2]
                    if s == 3:
                        t.wait_ge(SM["v_S2"], i * NBAT + 15 + pc)
                    elif i * 8 + j >= 2:
                        t.wait_ge(SM["v_pe"], i * 8 + j - 1)
                    t.wait_ge(SM["c_AVev"], i * NBAT + 4 * (s + 1))
                    for ch in range(8):
                        t.matmul(B[:],
                                 attn_fm[:, ch * 512 + s * 128: ch * 512 + (s + 1) * 128],
                                 wprojT[:, ch * 1024 + pc * 512: ch * 1024 + (pc + 1) * 512],
                                 start=(ch == 0), stop=False)
                    t.matmul(B[:], c_1r[0:1, :],
                             bproj[0:1, pc * 512:(pc + 1) * 512],
                             start=False, stop=True).then_inc(SM["t_pj"], 1)

                for bb in range(NBAT):
                    ib = i * NBAT + bb
                    Sb = P[4 + bb % 2]
                    BV = P[7] if (bb % 2 == 0 or bb == NBAT - 1) else P[1]
                    t.wait_ge(SM["v_ksw"], i * 8 + bb // 2 + 1)
                    if ib >= 2:
                        t.wait_ge(SM["v_S2"], ib - 1)
                    if i >= 1 and bb <= 1:
                        t.wait_ge(SM["v_pe"], 8 * i - 1 + bb)
                    for g4 in range(4):
                        G = bb * 4 + g4
                        qs = slice(G * 64, (G + 1) * 64)
                        co = g4 * 128
                        t.matmul(Sb[0:64, co:co + 64], qt[0:64, qs], kt[0:64, qs], start=True, stop=True)
                        t.matmul(Sb[0:64, co + 64:co + 128], qt[0:64, qs], ktx[0:64, qs], start=True, stop=True)
                        t.matmul(Sb[64:128, co:co + 64], qt[64:128, qs], ktx[64:128, qs], start=True, stop=True)
                        ins = t.matmul(Sb[64:128, co + 64:co + 128], qt[64:128, qs], kt[64:128, qs], start=True, stop=True)
                    ins.then_inc(SM["t_S"], 1)
                    if bb == 0 and i >= 1:
                        t.wait_ge(SM["c_V8ev"], ib)
                    if bb == NBAT - 1:
                        t.wait_ge(SM["c_V8ev"], ib)
                    elif ib >= 2:
                        t.wait_ge(SM["c_V8ev"], ib - 1)
                    if bb <= 1:
                        if i >= 1:
                            t.wait_ge(SM["c_AVev"], 16 * i)
                    elif bb == NBAT - 1:
                        t.wait_ge(SM["c_AVev"], ib - 4)
                    elif ib >= 6:
                        t.wait_ge(SM["c_AVev"], ib - 5)
                    for g4 in range(4):
                        G = bb * 4 + g4
                        qs = slice(G * 64, (G + 1) * 64)
                        t.matmul(BV[0:64, g4 * 64:(g4 + 1) * 64], vt[0:64, qs],
                                 c_id[0:64, 0:64], start=True, stop=True)
                        ins = t.matmul(BV[64:128, g4 * 64:(g4 + 1) * 64], vt[64:128, qs],
                                       c_id[64:128, 64:128], start=True, stop=True)
                    ins.then_inc(SM["t_V8"], 1)
                    if bb >= 3:
                        t_at(bb - 3)
                    if bb >= 4:
                        t_av(bb - 4)
                    if bb == 9:
                        t_proj(0, 0)
                        t_proj(0, 1)
                    elif bb == 12:
                        t_proj(1, 0)
                        t_proj(1, 1)
                    elif bb == 15:
                        t_proj(2, 0)
                        t_proj(2, 1)
                t_at(NBAT - 3)
                t_av(NBAT - 4)
                t_at(NBAT - 2)
                t_av(NBAT - 3)
                t_at(NBAT - 1)
                t_av(NBAT - 2)
                t_av(NBAT - 1)
                t_proj(3, 0)
                t_proj(3, 1)

            t.wait_ge(SM["s_wb1"], 32)
            ntot_t = nt * NSUB

            def xqT(k):
                t.wait_ge(SM["v_xq"], k + 1)
                for b in range(2):
                    if k >= 2:
                        t.wait_ge(SM["c_hqfm"], 8 * (k - 1))
                    for kk in range(4):
                        j = b * 4 + kk
                        ins = t.matmul(P[6 + b][:, kk * 128:(kk + 1) * 128],
                                       xq_tm[k % 3][:, j * 128:(j + 1) * 128], c_id[:],
                                       start=True, stop=True)
                    ins.then_inc(SM["t_xqT"], 1)
            xqT(0)

            def hqT(k, b):
                t.wait_ge(SM["v_hq"], 2 * k + (1 if b < 4 else 2))
                if b < 2 and k + 1 < ntot_t:
                    t.wait_ge(SM["c_xqfm"], 2 * (k + 1) + 1 + b)
                elif k * 8 + b >= 2:
                    t.wait_ge(SM["c_hqfm"], k * 8 + b - 1)
                for kk in range(4):
                    j = b * 4 + kk
                    ins = t.matmul(P[6 + b % 2][:, kk * 128:(kk + 1) * 128],
                                   hq_tm[:, j * 128:(j + 1) * 128], c_id[:],
                                   start=True, stop=True)
                ins.then_inc(SM["t_hqT"], 1)

            for i in range(nt):
                for s in range(NSUB):
                    k = i * NSUB + s
                    t.wait_ge(SM["c_xqfm"], 2 * (k + 1))
                    for pc in range(8):
                        jg = k * 8 + pc
                        if jg >= 3:
                            t.wait_ge(SM["c_g"], jg - 2)
                        for m in range(8):
                            ins = t.matmul(P[[2, 3, 0][jg % 3]][:],
                                           xq_fm[k % 2][:, m * 128:(m + 1) * 128],
                                           wgq[:, m * FI + pc * 512: m * FI + (pc + 1) * 512],
                                           start=(m == 0), stop=(m == 7))
                        ins.then_inc(SM["t_g"], 1)
                        if jg >= 3:
                            t.wait_ge(SM["c_u"], jg - 2)
                        for m in range(8):
                            ins = t.matmul(P[[4, 5, 1][jg % 3]][:],
                                           xq_fm[k % 2][:, m * 128:(m + 1) * 128],
                                           wuq[:, m * FI + pc * 512: m * FI + (pc + 1) * 512],
                                           start=(m == 0), stop=(m == 7))
                        ins.then_inc(SM["t_u"], 1)
                        if k >= 1:
                            hqT(k - 1, pc)
                    if k + 1 < ntot_t:
                        xqT(k + 1)
            for b in range(8):
                hqT(ntot_t - 1, b)

            t.wait_ge(SM["s_wb2"], 16)
            for i in range(nt):
                for s in range(NSUB):
                    k = i * NSUB + s
                    t.wait_ge(SM["s_hqia" if k % 2 == 0 else "s_hqib"], 16 * (k // 2 + 1))
                    for pc in range(2):
                        j = k * 2 + pc
                        if j >= 4:
                            t.wait_ge(SM["v_o"], j - 3)
                        for m in range(32):
                            ins = t.matmul(P[j % 4][:],
                                           hq_in[k % 2][:, m * 128:(m + 1) * 128],
                                           wdq[:, m * 1024 + pc * 512: m * 1024 + (pc + 1) * 512],
                                           start=(m == 0), stop=(m == 31))
                        ins.then_inc(SM["t_d"], 1)

        def scalar_body(sc):
            for j in range(NPC):
                _, _, w = piece_desc(j)
                sc.wait_ge(SM[f"s_stg{j % 2}"], 16 * (j // 2 + 1))
                if j > 0:
                    sc.drain()
                sc.activation(qout[0][:, 0:w], stage[j % 2][:, 0:w], AF.Abs,
                              accum_out=acc_sb[:, j:j + 1]).then_inc(SM["c_abs"], 1)

            def c_sqrt(i):
                sc.wait_ge(SM["v_st1"], i + 1)
                sc.activation(lns[:, 24:28], lns[:, 20:24], AF.Sqrt).then_inc(SM["c_sq"], 1)
            c_sqrt(0)
            for i in range(nt):

                for b in range(8):
                    s2, q = b // 2, b % 2
                    sc.wait_ge(SM["t_xnT"], i * 8 + b + 1)
                    dst = xn_fm[:].rearrange("p (mc t) -> p mc t", mc=8)[
                        :, q * 4:(q + 1) * 4, s2 * 128:(s2 + 1) * 128]
                    srcv = P[[2, 3, 0, 1][b % 4]][:].rearrange("p (mc4 t) -> p mc4 t", mc4=4)
                    sc.copy(dst, srcv).then_inc(SM["c_xnfm"], 1)
                for c in range(24):
                    sc.wait_ge(SM["t_qkv"], i * 24 + c + 1)
                    src = P[2 + c % 2][:].rearrange("p (G tb) -> p G tb", G=NGRP)
                    if c < 8:
                        dst = qt[:].rearrange("p (G c3 tb) -> p G c3 tb", G=NGRP, c3=8)[:, :, c, :]
                        ins = sc.mul(dst, src, 0.125)
                    elif c < 16:
                        dst = kt[:].rearrange("p (G c3 tb) -> p G c3 tb", G=NGRP, c3=8)[:, :, c - 8, :]
                        ins = sc.copy(dst, src)
                    else:
                        dst = vt[:].rearrange("p (G c3 tb) -> p G c3 tb", G=NGRP, c3=8)[:, :, c - 16, :]
                        ins = sc.copy(dst, src)
                    ins.then_inc(SM["c_qkvev"], 1)
                if i + 1 < nt:
                    c_sqrt(i + 1)

                def sc_atev(bb):
                    ib = i * NBAT + bb
                    BA = P[6] if (bb % 2 == 0 or bb == NBAT - 1) else P[0]
                    sc.wait_ge(SM["t_At"], ib + 1)
                    if ib >= 4:
                        sc.wait_ge(SM["t_AV"], ib - 3)
                    sc.copy(atsb[bb % 4][:], BA[:]).then_inc(SM["c_Atev"], 1)

                def sc_avev(bb):
                    ib = i * NBAT + bb
                    BV = P[7] if (bb % 2 == 0 or bb == NBAT - 1) else P[1]
                    sc.wait_ge(SM["t_AV"], ib + 1)
                    srcv = BV[:, 256:512].rearrange("p (g4 c8 tb) -> p g4 c8 tb", g4=4, c8=8)
                    dst = attn_fm[:].rearrange("p (c8 t) -> p c8 t", c8=8)[
                        :, :, bb * 32:(bb + 1) * 32].rearrange("p c8 (g4 tb) -> p g4 c8 tb", g4=4)
                    sc.copy(dst, srcv).then_inc(SM["c_AVev"], 1)

                for bb in range(NBAT):
                    ib = i * NBAT + bb
                    BV = P[7] if (bb % 2 == 0 or bb == NBAT - 1) else P[1]
                    mxc = 60 if bb % 2 == 0 else 48
                    if bb >= 3:
                        sc_atev(bb - 3)
                    acc = 64 if bb % 2 == 0 else 68
                    sc.wait_ge(SM["v_S2"], ib + 1)
                    for g4 in range(4):
                        ins = sc.activation(esb[bb % 2][:, g4 * 128:(g4 + 1) * 128],
                                            s1f[bb % 2][:, g4 * 128:(g4 + 1) * 128], AF.Exp,
                                            bias=lns[:, mxc + g4:mxc + g4 + 1],
                                            accum_out=lns[:, acc + g4:acc + g4 + 1])
                    ins.then_inc(SM["c_E"], 1)
                    sc.wait_ge(SM["t_V8"], ib + 1)
                    if ib >= 4:
                        sc.wait_ge(SM["t_AV"], ib - 3)
                    sc.copy(v8sb[bb % 4][:], BV[:, 0:256]).then_inc(SM["c_V8ev"], 1)
                    if bb >= 4:
                        sc_avev(bb - 4)
                sc_atev(NBAT - 3)
                sc_avev(NBAT - 4)
                sc_atev(NBAT - 2)
                sc_avev(NBAT - 3)
                sc_atev(NBAT - 1)
                sc_avev(NBAT - 2)
                sc_avev(NBAT - 1)

            ntot = nt * NSUB

            def sq2(kq):
                L = (kq % 2) * 48
                sc.wait_ge(SM["v_st2"], kq + 1)
                sc.activation(ln2[:, L + 3:L + 4], ln2[:, L + 2:L + 3], AF.Sqrt).then_inc(SM["c_sq2"], 1)
            sq2(0)
            if ntot > 1:
                sq2(1)

            def sc_hqev(kq, b):
                sc.wait_ge(SM["t_hqT"], kq * 8 + b + 1)
                if kq >= 2:
                    sc.wait_ge(SM[f"s_hqo{kq % 2}"], 16 * (kq // 2))
                sc.copy(hq_fm[kq % 2][:, b * 512:(b + 1) * 512],
                        P[6 + b % 2][:]).then_inc(SM["c_hqfm"], 1)

            def xq_evacs(k):
                if k >= 2:
                    sc.wait_ge(SM["t_u"], 8 * (k - 1))
                for b in range(2):
                    sc.wait_ge(SM["t_xqT"], k * 2 + b + 1)
                    sc.copy(xq_fm[k % 2][:, b * 512:(b + 1) * 512], P[6 + b][:]).then_inc(SM["c_xqfm"], 1)

            xq_evacs(0)
            for i in range(nt):
                for s in range(NSUB):
                    k = i * NSUB + s
                    if k + 2 < ntot:
                        sq2(k + 2)
                    for pc in range(8):
                        jg = k * 8 + pc
                        sc.wait_ge(SM["t_g"], jg + 1)
                        if jg >= 2:
                            sc.wait_ge(SM["v_M"], jg - 1)
                        sc.activation(silug[jg % 2][:], P[[2, 3, 0][jg % 3]][:], AF.Silu,
                                      scale=agm[:, k % 4:k % 4 + 1]).then_inc(SM["c_g"], 1)
                        sc.wait_ge(SM["t_u"], jg + 1)
                        sc.copy(u_sb[jg % 2][:], P[[4, 5, 1][jg % 3]][:]).then_inc(SM["c_u"], 1)
                        if k >= 1:
                            sc_hqev(k - 1, pc)
                    if k + 1 < ntot:
                        xq_evacs(k + 1)
            for b in range(8):
                sc_hqev(ntot - 1, b)

        def vector_body(v):
            TS = v.tensor_scalar
            TT = v.tensor_tensor
            STT = v.scalar_tensor_tensor

            v.wait_ge(SM["t_sums"], 1)
            v.tensor_copy(sums_sb[0:1, 0:NPC], P[0][0:1, 0:NPC])
            v.drain()
            v.reduce_sum(sums_sb[0:1, 48:49], sums_sb[0:1, 0:8], axis=AX.X)
            v.reduce_sum(sums_sb[0:1, 49:50], sums_sb[0:1, 8:16], axis=AX.X)
            v.reduce_sum(sums_sb[0:1, 50:51], sums_sb[0:1, 16:24], axis=AX.X)
            v.drain()
            TS(sums_sb[0:1, 51:53], sums_sb[0:1, 48:50], 1.0 / (D * FI), 1e-5, ALU.mult, ALU.max)
            TS(sums_sb[0:1, 53:54], sums_sb[0:1, 50:51], 1.0 / (FI * D), 1e-5, ALU.mult, ALU.max)
            v.drain()
            v.reciprocal(sums_sb[0:1, 8:11], sums_sb[0:1, 51:54]).then_inc(SM["v_scales"], 1)
            v.wait_ge(SM["g_bc"], 1)
            v.tensor_copy(wsc[:, 0:3], P[1][:, 0:3])
            v.drain()
            for j in range(NPC):
                _, _, w = piece_desc(j)
                v.wait_ge(SM["s_st2a" if j % 2 == 0 else "s_st2b"], 16 * (j // 2 + 1))
                if j >= 2:
                    v.wait_ge(SM[f"s_qo{j % 2}"], 16 * (j // 2))
                TS(qtmp[:, 0:w], stage[j % 2][:, 0:w], wsc[:, sw_col(j):sw_col(j) + 1],
                   MAGIC, ALU.mult, ALU.add).then_inc(SM["v_stg"], 1)
                v.drain()
                TS(qtmp[:, 0:w], qtmp[:, 0:w], MAGIC, None, ALU.subtract)
                v.drain()
                TS(qout[j % 2][:, 0:w], qtmp[:, 0:w], -1.0, 1.0, ALU.max, ALU.min).then_inc(SM["v_q"], 1)
                v.drain()

            def v_ln(i):
                xb = x_sb[i % 2]
                ro = 36 if i % 2 == 0 else 44
                v.wait_ge(SM[f"s_x{i % 2}"], 16 * (i // 2 + 1))
                for s2 in range(NSUB):
                    xs = xb[:, s2 * 1024:(s2 + 1) * 1024]
                    if s2 > 0:
                        v.drain()
                    v.bn_stats(lns[:, 0:6], xs[:, 0:512])
                    v.bn_stats(lns[:, 6:12], xs[:, 512:1024])
                    v.drain()
                    v.bn_aggr(lns[:, 12 + 2 * s2:14 + 2 * s2], lns[:, 0:12])
                v.drain()
                pairs = lns[:, 12:20].rearrange("p (s k) -> p s k", k=2)
                v.tensor_copy(lns[:, 52:56].rearrange("p s -> p s ()"), pairs[:, :, 0:1])
                v.tensor_copy(lns[:, 56:60].rearrange("p s -> p s ()"), pairs[:, :, 1:2])
                v.drain()
                TS(lns[:, 20:24], lns[:, 56:60], EPS, None, ALU.add).then_inc(SM["v_st1"], 1)
                v.wait_ge(SM["c_sq"], i + 1)
                v.reciprocal(lns[:, 28:32], lns[:, 24:28])
                v.drain()
                TT(lns[:, 32:36], lns[:, 28:32], lns[:, 28:32], ALU.mult)
                v.drain()
                TT(lns[:, 60:64], lns[:, 32:36], lns[:, 20:24], ALU.mult)
                v.drain()
                TS(lns[:, 32:36], lns[:, 60:64], -0.5, 1.5, ALU.mult, ALU.add)
                v.drain()
                TT(lns[:, ro:ro + 4], lns[:, 28:32], lns[:, 32:36], ALU.mult)
                v.drain()
                STT(lns[:, ro + 4:ro + 8], lns[:, 52:56], -1.0, lns[:, ro:ro + 4], ALU.mult, ALU.mult)
                v.drain()
                if i >= 1:
                    v.wait_ge(SM["t_xnT"], 8 * i)
                for s2 in range(NSUB):
                    ins = TS(xn_tm[:, s2 * 1024:(s2 + 1) * 1024],
                             xb[:, s2 * 1024:(s2 + 1) * 1024],
                             lns[:, ro + s2:ro + 1 + s2],
                             lns[:, ro + 4 + s2:ro + 5 + s2], ALU.mult, ALU.add)
                    if s2 == NSUB - 1:
                        ins.then_inc(SM["c_xn"], 1)

            v_ln(0)
            for i in range(nt):
                xb = x_sb[i % 2]
                if i + 1 < nt:
                    v_ln(i + 1)
                for pc in range(8):
                    v.wait_ge(SM["t_ksw"], i * 8 + pc + 1)
                    v.tensor_copy(ktx[:, pc * 512:(pc + 1) * 512], P[2 + pc % 2][:]).then_inc(SM["v_ksw"], 1)
                def v_s1(bb):
                    ib = i * NBAT + bb
                    Sb = P[4 + bb % 2]
                    mx = slice(60, 64) if bb % 2 == 0 else slice(48, 52)
                    v.wait_ge(SM["t_S"], ib + 1)
                    if ib >= 2:
                        v.wait_ge(SM["c_E"], ib - 1)
                    TT(s1f[bb % 2][:], Sb[:], c_mask[:], ALU.add)
                    v.drain()
                    v.reduce_max(lns[:, mx], s1f[bb % 2][:].rearrange("p (g c) -> p g c", g=4),
                                 axis=AX.X, negate=True).then_inc(SM["v_S2"], 1)

                def v_s2(bb):
                    ib = i * NBAT + bb
                    sm = slice(56, 60) if bb % 2 == 0 else slice(52, 56)
                    v.wait_ge(SM["c_E"], ib + 1)
                    acc2 = slice(64, 68) if bb % 2 == 0 else slice(68, 72)
                    v.reciprocal(lns[:, sm], lns[:, acc2])
                    v.drain()
                    if ib >= 4:
                        v.wait_ge(SM["t_At"], ib - 3)
                    TT(asb[bb % 4][:].rearrange("p (g c) -> p g c", g=4),
                       esb[bb % 2][:].rearrange("p (g c) -> p g c", g=4),
                       lns[:, sm].rearrange("p g -> p g ()").broadcast_to([128, 4, 128]),
                       ALU.mult).then_inc(SM["v_A"], 1)

                def v_pevac(s, pc):
                    j = s * 2 + pc
                    v.wait_ge(SM["t_pj"], i * 8 + j + 1)
                    if i == 1 and j == 0:
                        v.wait_ge(SM["s_qo0"], 16 * (NPC // 2))
                        v.wait_ge(SM["s_qo1"], 16 * (NPC // 2))
                    if i >= 2 and j == 0:
                        v.wait_ge(SM[f"s_xno{i % 2}"], 16 * (i // 2))
                    STT(xnew_sb[i % 2][:, s * 1024 + pc * 512: s * 1024 + (pc + 1) * 512],
                        (P[4 + pc] if s == 3 else P[2 + j % 2])[:], 1.0,
                        xb[:, s * 1024 + pc * 512: s * 1024 + (pc + 1) * 512],
                        ALU.mult, ALU.add).then_inc(SM["v_pe"], 1)

                for bb in range(NBAT):
                    v_s1(bb)
                    if bb >= 1:
                        v_s2(bb - 1)
                    if bb == 10:
                        v_pevac(0, 0)
                        v_pevac(0, 1)
                    elif bb == 13:
                        v_pevac(1, 0)
                        v_pevac(1, 1)
                v_s2(NBAT - 1)
                v_pevac(2, 0)
                v_pevac(2, 1)
                v_pevac(3, 0)
                v_pevac(3, 1)

            ntot = nt * NSUB

            def b1_stats(k):
                L = (k % 2) * 48
                xs = xnsub[k % 2]
                v.wait_ge(SM[f"s_xni{k % 2}"], 16 * (k // 2 + 1))
                if k >= 2:
                    v.wait_ge(SM["c_sq2"], k - 1)
                v.drain()
                v.bn_stats(ln2[:, L + 20:L + 26], xs[:, 0:512])
                v.bn_stats(ln2[:, L + 26:L + 32], xs[:, 512:1024])
                v.drain()
                v.bn_aggr(ln2[:, L + 0:L + 2], ln2[:, L + 20:L + 32])
                v.drain()
                TS(ln2[:, L + 2:L + 3], ln2[:, L + 1:L + 2], EPS, None, ALU.add)
                v.reduce_max(ln2[:, L + 7:L + 8], xs[:], axis=AX.X)
                v.tensor_reduce(ln2[:, L + 8:L + 9], xs[:], AX.X, ALU.min).then_inc(SM["v_st2"], 1)

            def b1_rest(k):
                L = (k % 2) * 48
                xs = xnsub[k % 2]
                v.wait_ge(SM["c_sq2"], k + 1)
                v.reciprocal(ln2[:, L + 4:L + 5], ln2[:, L + 3:L + 4])
                v.drain()
                TT(ln2[:, L + 5:L + 6], ln2[:, L + 4:L + 5], ln2[:, L + 4:L + 5], ALU.mult)
                v.drain()
                TT(ln2[:, L + 18:L + 19], ln2[:, L + 5:L + 6], ln2[:, L + 2:L + 3], ALU.mult)
                v.drain()
                TS(ln2[:, L + 19:L + 20], ln2[:, L + 18:L + 19], -0.5, 1.5, ALU.mult, ALU.add)
                v.drain()
                TT(ln2[:, L + 6:L + 7], ln2[:, L + 4:L + 5], ln2[:, L + 19:L + 20], ALU.mult)
                v.drain()
                TT(ln2[:, L + 9:L + 10], ln2[:, L + 7:L + 8], ln2[:, L + 0:L + 1], ALU.subtract)
                TT(ln2[:, L + 10:L + 11], ln2[:, L + 0:L + 1], ln2[:, L + 8:L + 9], ALU.subtract)
                v.drain()
                TT(ln2[:, L + 11:L + 12], ln2[:, L + 9:L + 10], ln2[:, L + 10:L + 11], ALU.max)
                v.drain()
                TT(ln2[:, L + 12:L + 13], ln2[:, L + 11:L + 12], ln2[:, L + 6:L + 7], ALU.mult)
                v.drain()
                TS(ln2[:, L + 13:L + 14], ln2[:, L + 12:L + 13], 1e-5, None, ALU.max)
                v.drain()
                v.reciprocal(ln2[:, L + 14:L + 15], ln2[:, L + 13:L + 14])
                v.drain()
                TS(ln2[:, L + 15:L + 16], ln2[:, L + 14:L + 15], 127.0, None, ALU.mult)
                v.drain()
                TT(ln2[:, L + 16:L + 17], ln2[:, L + 6:L + 7], ln2[:, L + 15:L + 16], ALU.mult)
                v.drain()
                STT(ln2[:, L + 17:L + 18], ln2[:, L + 0:L + 1], -1.0, ln2[:, L + 16:L + 17], ALU.mult, ALU.mult)
                v.drain()
                TS(ln2[:, L + 10:L + 11], ln2[:, L + 17:L + 18], 1.0, MAGIC, ALU.mult, ALU.add)
                if k >= 4:
                    v.wait_ge(SM["c_g"], 8 * (k - 3))
                TT(ln2[:, L + 18:L + 19], ln2[:, L + 15:L + 16], wsc[:, 0:1], ALU.mult)
                TT(ln2[:, L + 19:L + 20], ln2[:, L + 15:L + 16], wsc[:, 1:2], ALU.mult)
                v.drain()
                v.reciprocal(agm[:, k % 4:k % 4 + 1], ln2[:, L + 18:L + 19])
                v.reciprocal(agm[:, 4 + k % 4:5 + k % 4], ln2[:, L + 19:L + 20])
                if k >= 3:
                    v.wait_ge(SM["t_xqT"], 2 * (k - 2))
                TS(xqm[:], xs[:], ln2[:, L + 16:L + 17], ln2[:, L + 10:L + 11], ALU.mult, ALU.add)
                v.drain()
                TS(xq_tm[k % 3][:], xqm[:], MAGIC, None, ALU.subtract).then_inc(SM["v_xq"], 1)

            def b1_m(k):
                L = (k % 2) * 48
                for pc in range(8):
                    jg = k * 8 + pc
                    v.wait_ge(SM["c_u"], jg + 1)
                    v.wait_ge(SM["c_g"], jg + 1)
                    STT(m_sb[:, pc * 512:(pc + 1) * 512], u_sb[jg % 2][:],
                        agm[:, 4 + k % 4:5 + k % 4], silug[jg % 2][:],
                        ALU.mult, ALU.mult).then_inc(SM["v_M"], 1)
                    v.drain()
                    v.reduce_max(ln2[:, L + 32 + pc:L + 33 + pc],
                                 m_sb[:, pc * 512:(pc + 1) * 512],
                                 axis=AX.X, apply_absolute_value=True)
                v.drain()
                v.reduce_max(ln2[:, L + 40:L + 41], ln2[:, L + 32:L + 40], axis=AX.X)
                v.drain()
                TS(ln2[:, L + 41:L + 42], ln2[:, L + 40:L + 41], 1e-5, None, ALU.max)
                v.drain()
                v.reciprocal(ln2[:, L + 42:L + 43], ln2[:, L + 41:L + 42])
                v.drain()
                TS(ln2[:, L + 43:L + 44], ln2[:, L + 42:L + 43], 127.0, None, ALU.mult)
                v.drain()
                TT(ln2[:, L + 44:L + 45], ln2[:, L + 43:L + 44], wsc[:, 2:3], ALU.mult)
                v.drain()
                v.reciprocal(gam[:, k:k + 1], ln2[:, L + 44:L + 45])
                if k >= 1:
                    v.wait_ge(SM["t_hqT"], 8 * k)
                for hh in range(2):
                    hs = slice(hh * 2048, (hh + 1) * 2048)
                    TS(m_sb[:, hs], m_sb[:, hs], ln2[:, L + 43:L + 44], MAGIC, ALU.mult, ALU.add)
                    v.drain()
                    TS(hq_tm[:, hs], m_sb[:, hs], MAGIC, None, ALU.subtract).then_inc(SM["v_hq"], 1)
                    v.drain()

            b1_stats(0)
            b1_rest(0)
            if ntot > 1:
                b1_stats(1)
                b1_rest(1)
            for k in range(ntot):
                if k + 2 < ntot:
                    b1_stats(k + 2)
                    b1_rest(k + 2)
                b1_m(k)

            for i in range(nt):
                for s in range(NSUB):
                    k = i * NSUB + s
                    v.wait_ge(SM["s_xnia" if k % 2 == 0 else "s_xnib"], 16 * (k // 2 + 1))
                    for pc in range(2):
                        j = k * 2 + pc
                        v.wait_ge(SM["t_d"], j + 1)
                        if k >= 2 and pc == 0:
                            v.wait_ge(SM[f"s_out{k % 2}"], 16 * (k // 2))
                        STT(out_sb[k % 2][:, pc * 512:(pc + 1) * 512],
                            P[j % 4][:], gam[:, k:k + 1],
                            xnsub2[k % 2][:, pc * 512:(pc + 1) * 512],
                            ALU.mult, ALU.add).then_inc(SM["v_o"], 1)

        block.sync(sync_body)
        block.gpsimd(gpsimd_body)
        block.tensor(tensor_body)
        block.scalar(scalar_body)
        block.vector(vector_body)

    return nc


def _host_layouts(Wqkv, Wproj, bproj, w_gate, w_up, w_down):
    H, DH = 16, 64
    perm = np.empty(3 * D, dtype=np.int64)
    idx = 0
    for blk, off in ((0, 0), (1, 64), (2, 128)):
        for c in range(8):
            for par in range(2):
                h = 2 * c + par
                base = h * 192 + off
                perm[idx:idx + DH] = np.arange(base, base + DH)
                idx += DH
    wqkvT = np.ascontiguousarray(Wqkv[perm].T)
    hd_perm = np.empty(D, dtype=np.int64)
    idx = 0
    for c in range(8):
        for par in range(2):
            h = 2 * c + par
            hd_perm[idx:idx + DH] = np.arange(h * DH, (h + 1) * DH)
            idx += DH
    wprojT = np.ascontiguousarray(Wproj.T[hd_perm])
    bproj2 = np.ascontiguousarray(bproj.reshape(1, D))
    wgT = np.ascontiguousarray(w_gate.T)
    wuT = np.ascontiguousarray(w_up.T)
    wdT = np.ascontiguousarray(w_down.T)
    return wqkvT, wprojT, bproj2, wgT, wuT, wdT


def kernel(L0_attn_x, L0_attn_Wqkv, L0_attn_Wproj, L0_attn_bproj,
           L0_ffn_w_gate, L0_ffn_w_up, L0_ffn_w_down):
    x = np.ascontiguousarray(np.asarray(L0_attn_x, dtype=np.float32))
    B, N, _ = x.shape
    xf = x.reshape(B * N, D)
    wqkvT, wprojT, bproj2, wgT, wuT, wdT = _host_layouts(
        np.asarray(L0_attn_Wqkv, np.float32), np.asarray(L0_attn_Wproj, np.float32),
        np.asarray(L0_attn_bproj, np.float32), np.asarray(L0_ffn_w_gate, np.float32),
        np.asarray(L0_ffn_w_up, np.float32), np.asarray(L0_ffn_w_down, np.float32))
    per_core = (B * N) // N_CORES
    nt = per_core // TOK
    nc = build_nc(nt)
    in_maps = []
    for c in range(N_CORES):
        in_maps.append({
            "x": np.ascontiguousarray(xf[c * per_core:(c + 1) * per_core]),
            "wqkvT": wqkvT, "wprojT": wprojT, "bproj": bproj2,
            "wgT": wgT, "wuT": wuT, "wdT": wdT,
        })
    res = run_bass_kernel_spmd(nc, in_maps, core_ids=list(range(N_CORES)))
    out = np.concatenate([res.results[c]["out"] for c in range(N_CORES)], axis=0)
    return out.reshape(B, N, D).astype(np.float32)


# version 34
# speedup vs baseline: 1.0262x; 1.0141x over previous
import numpy as np
import ml_dtypes
import concourse.bass as bass
import concourse.mybir as mybir
from concourse.bass_utils import run_bass_kernel_spmd
from contextlib import ExitStack

F32 = mybir.dt.float32
BF16 = mybir.dt.bfloat16
ALU = mybir.AluOpType
AF = mybir.ActivationFunctionType
AX = mybir.AxisListType

MAGIC = float(np.float32(1.5 * 2.0 ** 23))
NEG = -30080.0
EPS = 1e-5
D = 1024
FI = 4096
TOK = 512
NSUB = 4
NGRP = TOK // 8
NBAT = NGRP // 4
NPC = 24
N_CORES = 8


def _consts():
    ident = np.eye(128, dtype=ml_dtypes.bfloat16)
    swap = np.zeros((128, 128), dtype=ml_dtypes.bfloat16)
    for i in range(128):
        swap[(i + 64) % 128, i] = 1.0
    mask = np.full((128, 512), NEG, dtype=ml_dtypes.bfloat16)
    r = np.arange(128) % 8
    c = np.arange(512) % 8
    mask[r[:, None] == c[None, :]] = 0.0
    ones_row = np.ones((1, 128), dtype=ml_dtypes.bfloat16)
    ones_col = np.ones((128, 1), dtype=np.float32)
    return ident, swap, mask, ones_row, ones_col


def build_nc(nt):
    n_tok = nt * TOK
    nc = bass.Bass("TRN2", target_bir_lowering=False)

    x_h = nc.dram_tensor("x", [n_tok, D], F32, kind="ExternalInput").ap()
    wqkvT_h = nc.dram_tensor("wqkvT", [D, 3 * D], F32, kind="ExternalInput").ap()
    wprojT_h = nc.dram_tensor("wprojT", [D, D], F32, kind="ExternalInput").ap()
    bproj_h = nc.dram_tensor("bproj", [1, D], F32, kind="ExternalInput").ap()
    wgT_h = nc.dram_tensor("wgT", [D, FI], F32, kind="ExternalInput").ap()
    wuT_h = nc.dram_tensor("wuT", [D, FI], F32, kind="ExternalInput").ap()
    wdT_h = nc.dram_tensor("wdT", [FI, D], F32, kind="ExternalInput").ap()
    out_h = nc.dram_tensor("out", [n_tok, D], F32, kind="ExternalOutput").ap()

    xnew_h = nc.dram_tensor("xnew_scr", [n_tok, D], F32).ap()
    hq_h = nc.dram_tensor("hq_scr", [nt * NSUB, 128, FI], BF16).ap()
    wgq_h = nc.dram_tensor("wgq_scr", [D, FI], BF16).ap()
    wuq_h = nc.dram_tensor("wuq_scr", [D, FI], BF16).ap()
    wdq_h = nc.dram_tensor("wdq_scr", [FI, D], BF16).ap()

    ident_np, swap_np, mask_np, ones_row_np, ones_col_np = _consts()
    ident_h = nc.inline_tensor(ident_np, name="c_ident_h").ap()
    swap_h = nc.inline_tensor(swap_np, name="c_swap_h").ap()
    mask_h = nc.inline_tensor(mask_np, name="c_mask_h").ap()
    onesr_h = nc.inline_tensor(ones_row_np, name="c_onesr_h").ap()
    onesc_h = nc.inline_tensor(ones_col_np, name="c_onesc_h").ap()
    onesrf_h = nc.inline_tensor(np.ones((1, 128), dtype=np.float32), name="c_onesrf_h").ap()

    def piece_desc(j):
        if j < 8:
            return wgT_h[j * 128:(j + 1) * 128, :], wgq_h[j * 128:(j + 1) * 128, :], FI
        if j < 16:
            k = j - 8
            return wuT_h[k * 128:(k + 1) * 128, :], wuq_h[k * 128:(k + 1) * 128, :], FI
        k = j - 16
        src = wdT_h[k * 512:(k + 1) * 512, :].rearrange("(s p) f -> p s f", p=128)
        dst = wdq_h[k * 512:(k + 1) * 512, :].rearrange("(s p) f -> p s f", p=128)
        return src, dst, FI

    def sw_col(j):
        return 0 if j < 8 else (1 if j < 16 else 2)

    with ExitStack() as ctx:
        e = ctx.enter_context
        c_id = e(nc.sbuf_tensor("c_id", [128, 128], BF16))
        c_sw = e(nc.sbuf_tensor("c_sw", [128, 128], BF16))
        c_1r = e(nc.sbuf_tensor("c_1r", [1, 128], BF16))
        c_1c = e(nc.sbuf_tensor("c_1c", [128, 1], F32))
        c_1rf = e(nc.sbuf_tensor("c_1rf", [1, 128], F32))
        wsc = e(nc.sbuf_tensor("wsc", [128, 4], F32))
        sums_sb = e(nc.sbuf_tensor("sums_sb", [1, 64], F32))
        acc_sb = e(nc.sbuf_tensor("acc_sb", [128, NPC], F32))
        gam = e(nc.sbuf_tensor("gam", [128, 64], F32))
        agm = e(nc.sbuf_tensor("agm", [128, 8], F32))
        lns = e(nc.sbuf_tensor("lns", [128, 72], F32))
        ln2 = e(nc.sbuf_tensor("ln2", [128, 96], F32))

        P = [e(nc.psum_tensor(f"pb{i}", [128, 512], F32)) for i in range(8)]

        sem_names = [
            "s_wa", "s_wag", "s_stg0", "s_stg1", "s_st2a", "s_st2b", "s_qo0", "s_qo1", "s_x0", "s_x1", "s_wb1", "s_wb2",
            "s_xno0", "s_xno1", "s_xni0", "s_xni1", "s_xni2", "s_hqo0", "s_hqo1", "s_hqia", "s_hqib", "s_xnia", "s_xnib",
            "s_out0", "s_out1", "g_bc",
            "c_abs", "t_sums", "v_scales", "v_stg", "v_q",
            "v_st1", "c_sq", "v_ln", "c_xn", "t_xnT", "c_xnfm",
            "t_qkv", "c_qkvev", "t_ksw", "v_ksw",
            "t_S", "v_S2", "c_E", "v_A",
            "t_At", "c_Atev", "t_V8", "c_V8ev", "t_AV", "c_AVev",
            "t_pj", "v_pe",
            "v_st2", "c_sq2", "v_xq", "t_xqT", "c_xqfm",
            "t_g", "c_g", "t_u", "c_u", "v_M", "v_hq", "t_hqT", "c_hqfm",
            "t_d", "v_o",
        ]
        SM = {n: e(nc.semaphore(n)) for n in sem_names}
        DU = {}

        def dmax(eng, out, in_, semname):
            ins = eng.dma_start(out=out, in_=in_).then_inc(SM[semname], 16)
            tot = 0
            for u in ins.ins.sync.on_update:
                if u.id == SM[semname].num:
                    tot += u.update_value
            if semname in DU:
                assert DU[semname] == tot, (semname, DU[semname], tot)
            else:
                DU[semname] = tot
            return ins

        def du(semname):
            return DU[semname]

        s1f0 = e(nc.sbuf_tensor("s1f0", [128, 512], F32))
        esb0 = e(nc.sbuf_tensor("esb0", [128, 512], F32))
        asb0 = e(nc.sbuf_tensor("asb0", [128, 512], BF16))
        atsb0 = e(nc.sbuf_tensor("atsb0", [128, 512], BF16))
        v8sb0 = e(nc.sbuf_tensor("v8sb0", [128, 256], BF16))

        AB = (nc.sbuf_base + 31) // 32 * 32
        KB = 1024

        def at(name, shape, dtype, off):
            assert AB + off + _bytes(shape, dtype) <= 229344, (name, off)
            return nc.alloc_sbuf_tensor_at(name, shape, dtype, offset=AB + off)

        def _bytes(shape, dtype):
            esz = 4 if dtype == F32 else 2
            return shape[1] * esz

        wqkvT = at("wqkvT_sb", [128, 8 * 3072], BF16, 0)
        wprojT = at("wprojT_sb", [128, 8 * 1024], BF16, 48 * KB)
        x_sb = [at(f"x_sb{i}", [128, 4096], F32, 64 * KB + 16 * KB * i) for i in range(2)]
        xn_tm = at("xn_tm", [128, 4096], BF16, 96 * KB)
        xn_fm = at("xn_fm", [128, 4096], BF16, 104 * KB)
        attn_fm = at("attn_fm", [128, 4096], BF16, 112 * KB)
        qt = at("qt", [128, 4096], BF16, 120 * KB)
        kt = at("kt", [128, 4096], BF16, 128 * KB)
        vt = at("vt", [128, 4096], BF16, 136 * KB)
        ktx = at("ktx", [128, 4096], BF16, 144 * KB)
        xnew_sb = [at(f"xnew{i}", [128, 4096], F32, 152 * KB + 16 * KB * i) for i in range(2)]
        c_mask = at("c_mask", [128, 512], BF16, 190 * KB + 512)
        bproj = at("bproj_sb", [1, 1024], BF16, 196 * KB + 512)
        s1f1 = at("s1f1", [128, 512], F32, 184 * KB)
        esb1 = at("esb1", [128, 512], F32, 186 * KB)
        asb1 = at("asb1", [128, 512], BF16, 188 * KB)
        atsb1 = at("atsb1", [128, 512], BF16, 189 * KB)
        v8sb1 = at("v8sb1", [128, 256], BF16, 190 * KB)
        asb2 = at("asb2", [128, 512], BF16, 191 * KB + 512)
        asb3 = at("asb3", [128, 512], BF16, 192 * KB + 512)
        atsb2 = at("atsb2", [128, 512], BF16, 193 * KB + 512)
        atsb3 = at("atsb3", [128, 512], BF16, 194 * KB + 512)
        v8sb2 = at("v8sb2", [128, 256], BF16, 195 * KB + 512)
        v8sb3 = at("v8sb3", [128, 256], BF16, 196 * KB)
        s1f = [s1f0, s1f1]
        esb = [esb0, esb1]
        asb = [asb0, asb1, asb2, asb3]
        atsb = [atsb0, atsb1, atsb2, atsb3]
        v8sb = [v8sb0, v8sb1, v8sb2, v8sb3]
        stage = [at(f"stage{i}", [128, FI], F32, 120 * KB + 16 * KB * i) for i in range(2)]
        qtmp = at("qtmp", [128, FI], F32, 152 * KB)
        qout = [at(f"qout{i}", [128, FI], BF16, 168 * KB + 8 * KB * i) for i in range(2)]
        wgq = at("wgq_sb", [128, 8 * FI], BF16, 0)
        wuq = at("wuq_sb", [128, 8 * FI], BF16, 64 * KB)
        xnsub = [at(f"xnsub{i}", [128, 1024], F32, 128 * KB + 4 * KB * i) for i in range(2)]
        xqm = at("xqm", [128, 1024], F32, 136 * KB)
        xq_tm = [at(f"xq_tm{i}", [128, 1024], BF16, 140 * KB + 2 * KB * i) for i in range(3)]
        xq_fm = [at(f"xq_fm{i}", [128, 1024], BF16, 146 * KB + 2 * KB * i) for i in range(2)]
        silug = [at(f"silug{i}", [128, 512], F32, 150 * KB + 2 * KB * i) for i in range(2)]
        u_sb = [at(f"u_sb{i}", [128, 512], F32, 154 * KB + 2 * KB * i) for i in range(2)]
        m_sb = at("m_sb", [128, FI], F32, 158 * KB)
        hq_tm = at("hq_tm", [128, FI], BF16, 174 * KB)
        hq_fm = [at(f"hq_fm{i}", [128, FI], BF16, 182 * KB + 8 * KB * i) for i in range(2)]
        wdq = at("wdq_sb", [128, 32 * 1024], BF16, 0)
        hq_in = [at(f"hq_in{i}", [128, FI], BF16, 64 * KB + 8 * KB * i) for i in range(2)]
        xnsub2 = [at(f"xnsub2{i}", [128, 1024], F32, 80 * KB + 4 * KB * i) for i in range(2)]
        out_sb = [at(f"out_sb{i}", [128, 1024], F32, 88 * KB + 4 * KB * i) for i in range(2)]

        block = e(nc.Block())

        def sync_body(sync):
            sync.dma_start(out=c_id[:], in_=ident_h).then_inc(SM["s_wa"], 16)
            sync.dma_start(out=c_sw[:], in_=swap_h).then_inc(SM["s_wa"], 16)
            sync.dma_start(out=c_mask[:], in_=mask_h).then_inc(SM["s_wa"], 16)
            sync.dma_start(out=c_1r[:], in_=onesr_h).then_inc(SM["s_wa"], 16)
            sync.dma_start(out=c_1c[:], in_=onesc_h).then_inc(SM["s_wa"], 16)
            sync.dma_start(out=c_1rf[:], in_=onesrf_h).then_inc(SM["s_wa"], 16)
            for j in range(NPC):
                psrc, _, w = piece_desc(j)
                dstv = stage[j % 2][:, 0:w]
                if j >= 16:
                    dstv = dstv.rearrange("p (s f) -> p s f", s=4)
                if j >= 2:
                    sync.wait_ge(SM["c_abs"], j - 1)
                sync.dma_start(out=dstv, in_=psrc).then_inc(SM[f"s_stg{j % 2}"], 16)
            sync.wait_ge(SM["c_abs"], NPC)
            for j in range(NPC):
                psrc, _, w = piece_desc(j)
                dstv = stage[j % 2][:, 0:w]
                if j >= 16:
                    dstv = dstv.rearrange("p (s f) -> p s f", s=4)
                if j >= 2:
                    sync.wait_ge(SM["v_stg"], j - 1)
                sync.dma_start(out=dstv, in_=psrc).then_inc(SM["s_st2a" if j % 2 == 0 else "s_st2b"], 16)
            for i in range(nt):
                if i >= 2:
                    sync.wait_ge(SM["v_pe"], 8 * (i - 1))
                src = x_h[i * TOK:(i + 1) * TOK, :].rearrange("(s p) d -> p s d", p=128)
                dst = x_sb[i % 2][:].rearrange("p (s d) -> p s d", s=NSUB)
                sync.dma_start(out=dst, in_=src).then_inc(SM[f"s_x{i % 2}"], 16)
            sync.wait_ge(SM["s_qo0"], 16 * (NPC // 2))
            sync.wait_ge(SM["s_qo1"], 16 * (NPC // 2))
            sync.wait_ge(SM["v_pe"], 8 * nt)
            sync.wait_ge(SM["c_AVev"], 16 * nt)
            sync.wait_ge(SM["s_xno0"], 16 * ((nt + 1) // 2))
            sync.wait_ge(SM["s_xno1"], 16 * (nt // 2))
            sync.dma_start(out=wgq[:].rearrange("p (m f) -> p m f", m=8),
                           in_=wgq_h.rearrange("(m p) f -> p m f", p=128)).then_inc(SM["s_wb1"], 16)
            sync.dma_start(out=wuq[:].rearrange("p (m f) -> p m f", m=8),
                           in_=wuq_h.rearrange("(m p) f -> p m f", p=128)).then_inc(SM["s_wb1"], 16)
            for i in range(nt):
                for s in range(NSUB):
                    k = i * NSUB + s
                    if k >= 2:
                        sync.wait_ge(SM["v_xq"], k - 1)
                    sync.wait_ge(SM[f"s_xno{i % 2}"], 16 * (i // 2 + 1))
                    src = xnew_h[i * TOK + s * 128: i * TOK + (s + 1) * 128, :]
                    sync.dma_start(out=xnsub[k % 2][:], in_=src).then_inc(SM[f"s_xni{k % 2}"], 16)
            sync.wait_ge(SM["s_hqo0"], 16 * (nt * NSUB // 2))
            sync.wait_ge(SM["s_hqo1"], 16 * (nt * NSUB // 2))
            sync.dma_start(out=wdq[:].rearrange("p (m f) -> p m f", m=32),
                           in_=wdq_h.rearrange("(m p) f -> p m f", p=128)).then_inc(SM["s_wb2"], 16)
            for i in range(nt):
                for s in range(NSUB):
                    k = i * NSUB + s
                    if k >= 2:
                        sync.wait_ge(SM["v_o"], 2 * (k - 1))
                    src = xnew_h[i * TOK + s * 128: i * TOK + (s + 1) * 128, :]
                    sync.dma_start(out=xnsub2[k % 2][:], in_=src).then_inc(SM["s_xnia" if k % 2 == 0 else "s_xnib"], 16)
                    if k >= 2:
                        sync.wait_ge(SM["t_d"], 2 * (k - 1))
                    sync.wait_ge(SM[f"s_hqo{k % 2}"], 16 * (k // 2 + 1))
                    sync.dma_start(out=hq_in[k % 2][:], in_=hq_h[k, :, :]).then_inc(SM["s_hqia" if k % 2 == 0 else "s_hqib"], 16)

        def gpsimd_body(g):
            g.dma_start(out=wqkvT[:].rearrange("p (m f) -> p m f", m=8),
                        in_=wqkvT_h.rearrange("(m p) f -> p m f", p=128)).then_inc(SM["s_wag"], 16)
            g.dma_start(out=wprojT[:].rearrange("p (m f) -> p m f", m=8),
                        in_=wprojT_h.rearrange("(m p) f -> p m f", p=128)).then_inc(SM["s_wag"], 16)
            g.dma_start(out=bproj[:], in_=bproj_h).then_inc(SM["s_wag"], 16)


            for j in range(NPC):
                psrc, pdst, w = piece_desc(j)
                outv = qout[j % 2][:, 0:w]
                if j >= 16:
                    outv = outv.rearrange("p (s f) -> p s f", s=4)
                g.wait_ge(SM["v_q"], j + 1)
                g.dma_start(out=pdst, in_=outv).then_inc(SM[f"s_qo{j % 2}"], 16)
            for i in range(nt):
                g.wait_ge(SM["v_pe"], 8 * (i + 1))
                dst = xnew_h[i * TOK:(i + 1) * TOK, :].rearrange("(s p) d -> p s d", p=128)
                srcv = xnew_sb[i % 2][:].rearrange("p (s d) -> p s d", s=NSUB)
                g.dma_start(out=dst, in_=srcv).then_inc(SM[f"s_xno{i % 2}"], 16)
            for i in range(nt):
                for s in range(NSUB):
                    k = i * NSUB + s
                    g.wait_ge(SM["c_hqfm"], 8 * (k + 1))
                    g.dma_start(out=hq_h[k, :, :], in_=hq_fm[k % 2][:]).then_inc(SM[f"s_hqo{k % 2}"], 16)
            for i in range(nt):
                for s in range(NSUB):
                    k = i * NSUB + s
                    g.wait_ge(SM["v_o"], 2 * (k + 1))
                    dst = out_h[i * TOK + s * 128: i * TOK + (s + 1) * 128, :]
                    g.dma_start(out=dst, in_=out_sb[k % 2][:]).then_inc(SM[f"s_out{k % 2}"], 16)
            g.wait_ge(SM["s_out0"], 16 * (nt * NSUB // 2))
            g.wait_ge(SM["s_out1"], 16 * (nt * NSUB // 2))

        def tensor_body(t):
            t.wait_ge(SM["c_abs"], NPC)
            t.wait_ge(SM["s_wa"], 16 * 6)
            t.wait_ge(SM["s_wag"], 16 * 3)
            t.matmul(P[0][0:1, 0:NPC], c_1c[:], acc_sb[:], start=True, stop=True).then_inc(SM["t_sums"], 1)
            t.wait_ge(SM["v_scales"], 1)
            t.matmul(P[1][:, 0:3], c_1rf[0:1, :], sums_sb[0:1, 8:11], start=True, stop=True).then_inc(SM["g_bc"], 1)

            for i in range(nt):
                t.wait_ge(SM["c_xn"], i + 1)
                for b in range(8):
                    s2, q = b // 2, b % 2
                    if i >= 1 and b < 4:
                        if b % 4 == 0:
                            t.wait_ge(SM["v_pe"], 8 * i - 3)
                        elif b % 4 == 1:
                            t.wait_ge(SM["v_pe"], 8 * i - 2)
                        elif b % 4 == 2:
                            t.wait_ge(SM["c_Atev"], 16 * i - 2)
                        else:
                            t.wait_ge(SM["c_AVev"], 16 * i - 2)
                    if i * 8 + b >= 4:
                        t.wait_ge(SM["c_xnfm"], i * 8 + b - 3)
                    for kk in range(4):
                        mc = q * 4 + kk
                        ins = t.matmul(P[[2, 3, 0, 1][b % 4]][:, kk * 128:(kk + 1) * 128],
                                       xn_tm[:, s2 * 1024 + mc * 128: s2 * 1024 + (mc + 1) * 128],
                                       c_id[:], start=True, stop=True)
                    ins.then_inc(SM["t_xnT"], 1)
                t.wait_ge(SM["c_xnfm"], 8 * (i + 1))
                if i >= 1:
                    t.wait_ge(SM["v_ksw"], 8 * i)
                for c in range(24):
                    if i * 24 + c >= 2:
                        t.wait_ge(SM["c_qkvev"], i * 24 + c - 1)
                    for m in range(8):
                        ins = t.matmul(P[2 + c % 2][:],
                                       wqkvT[:, m * 3072 + c * 128: m * 3072 + (c + 1) * 128],
                                       xn_fm[:, m * 512:(m + 1) * 512],
                                       start=(m == 0), stop=(m == 7))
                    ins.then_inc(SM["t_qkv"], 1)
                t.wait_ge(SM["c_qkvev"], 24 * (i + 1))
                for pc in range(8):
                    if i * 8 + pc >= 2:
                        t.wait_ge(SM["v_ksw"], i * 8 + pc - 1)
                    t.matmul(P[2 + pc % 2][:], c_sw[:], kt[:, pc * 512:(pc + 1) * 512],
                             start=True, stop=True).then_inc(SM["t_ksw"], 1)

                def t_at(bb):
                    ib = i * NBAT + bb
                    BA = P[6] if (bb % 2 == 0 or bb == NBAT - 1) else P[0]
                    t.wait_ge(SM["v_A"], ib + 1)
                    if bb == 0 and i >= 1:
                        t.wait_ge(SM["c_Atev"], ib)
                    if bb == NBAT - 1:
                        t.wait_ge(SM["c_Atev"], ib)
                    elif ib >= 2:
                        t.wait_ge(SM["c_Atev"], ib - 1)
                    for g4 in range(4):
                        ins = t.matmul(BA[:, g4 * 128:(g4 + 1) * 128],
                                       asb[bb % 4][:, g4 * 128:(g4 + 1) * 128], c_id[:],
                                       start=True, stop=True)
                    ins.then_inc(SM["t_At"], 1)

                def t_av(bb):
                    ib = i * NBAT + bb
                    BV = P[7] if (bb % 2 == 0 or bb == NBAT - 1) else P[1]
                    t.wait_ge(SM["c_Atev"], ib + 1)
                    t.wait_ge(SM["c_V8ev"], ib + 1)
                    if bb == NBAT - 4:
                        t.wait_ge(SM["c_V8ev"], i * NBAT + NBAT)
                    if bb == 0 and i >= 1:
                        t.wait_ge(SM["c_AVev"], ib)
                    if bb == NBAT - 1:
                        t.wait_ge(SM["c_AVev"], ib)
                    elif ib >= 2:
                        t.wait_ge(SM["c_AVev"], ib - 1)
                    for g4 in range(4):
                        vs = slice(g4 * 64, (g4 + 1) * 64)
                        t.matmul(BV[0:64, 256 + g4 * 64: 256 + (g4 + 1) * 64],
                                 v8sb[bb % 4][:, vs], atsb[bb % 4][:, g4 * 128: g4 * 128 + 64],
                                 start=True, stop=True)
                        ins = t.matmul(BV[64:128, 256 + g4 * 64: 256 + (g4 + 1) * 64],
                                       v8sb[bb % 4][:, vs], atsb[bb % 4][:, g4 * 128 + 64: (g4 + 1) * 128],
                                       start=True, stop=True)
                    ins.then_inc(SM["t_AV"], 1)

                def t_proj(s, pc):
                    j = s * 2 + pc
                    B = P[4 + pc] if s == 3 else P[2 + j % 2]
                    if s == 3:
                        t.wait_ge(SM["v_S2"], i * NBAT + 15 + pc)
                    elif i * 8 + j >= 2:
                        t.wait_ge(SM["v_pe"], i * 8 + j - 1)
                    t.wait_ge(SM["c_AVev"], i * NBAT + 4 * (s + 1))
                    for ch in range(8):
                        t.matmul(B[:],
                                 attn_fm[:, ch * 512 + s * 128: ch * 512 + (s + 1) * 128],
                                 wprojT[:, ch * 1024 + pc * 512: ch * 1024 + (pc + 1) * 512],
                                 start=(ch == 0), stop=False)
                    t.matmul(B[:], c_1r[0:1, :],
                             bproj[0:1, pc * 512:(pc + 1) * 512],
                             start=False, stop=True).then_inc(SM["t_pj"], 1)

                for bb in range(NBAT):
                    ib = i * NBAT + bb
                    Sb = P[4 + bb % 2]
                    BV = P[7] if (bb % 2 == 0 or bb == NBAT - 1) else P[1]
                    t.wait_ge(SM["v_ksw"], i * 8 + bb // 2 + 1)
                    if ib >= 2:
                        t.wait_ge(SM["v_S2"], ib - 1)
                    if i >= 1 and bb <= 1:
                        t.wait_ge(SM["v_pe"], 8 * i - 1 + bb)
                    for g4 in range(4):
                        G = bb * 4 + g4
                        qs = slice(G * 64, (G + 1) * 64)
                        co = g4 * 128
                        t.matmul(Sb[0:64, co:co + 64], qt[0:64, qs], kt[0:64, qs], start=True, stop=True)
                        t.matmul(Sb[0:64, co + 64:co + 128], qt[0:64, qs], ktx[0:64, qs], start=True, stop=True)
                        t.matmul(Sb[64:128, co:co + 64], qt[64:128, qs], ktx[64:128, qs], start=True, stop=True)
                        ins = t.matmul(Sb[64:128, co + 64:co + 128], qt[64:128, qs], kt[64:128, qs], start=True, stop=True)
                    ins.then_inc(SM["t_S"], 1)
                    if bb == 0 and i >= 1:
                        t.wait_ge(SM["c_V8ev"], ib)
                    if bb == NBAT - 1:
                        t.wait_ge(SM["c_V8ev"], ib)
                    elif ib >= 2:
                        t.wait_ge(SM["c_V8ev"], ib - 1)
                    if bb <= 1:
                        if i >= 1:
                            t.wait_ge(SM["c_AVev"], 16 * i)
                    elif bb == NBAT - 1:
                        t.wait_ge(SM["c_AVev"], ib - 4)
                    elif ib >= 6:
                        t.wait_ge(SM["c_AVev"], ib - 5)
                    for g4 in range(4):
                        G = bb * 4 + g4
                        qs = slice(G * 64, (G + 1) * 64)
                        t.matmul(BV[0:64, g4 * 64:(g4 + 1) * 64], vt[0:64, qs],
                                 c_id[0:64, 0:64], start=True, stop=True)
                        ins = t.matmul(BV[64:128, g4 * 64:(g4 + 1) * 64], vt[64:128, qs],
                                       c_id[64:128, 64:128], start=True, stop=True)
                    ins.then_inc(SM["t_V8"], 1)
                    if bb >= 3:
                        t_at(bb - 3)
                    if bb >= 4:
                        t_av(bb - 4)
                    if bb == 9:
                        t_proj(0, 0)
                        t_proj(0, 1)
                    elif bb == 12:
                        t_proj(1, 0)
                        t_proj(1, 1)
                t_at(NBAT - 3)
                t_av(NBAT - 4)
                t_proj(2, 0)
                t_proj(2, 1)
                t_at(NBAT - 2)
                t_av(NBAT - 3)
                t_at(NBAT - 1)
                t_av(NBAT - 2)
                t_av(NBAT - 1)
                t_proj(3, 0)
                t_proj(3, 1)

            t.wait_ge(SM["s_wb1"], 32)
            ntot_t = nt * NSUB

            def xqT(k):
                t.wait_ge(SM["v_xq"], k + 1)
                for b in range(2):
                    if k >= 2:
                        t.wait_ge(SM["c_hqfm"], 8 * (k - 1))
                    for kk in range(4):
                        j = b * 4 + kk
                        ins = t.matmul(P[6 + b][:, kk * 128:(kk + 1) * 128],
                                       xq_tm[k % 3][:, j * 128:(j + 1) * 128], c_id[:],
                                       start=True, stop=True)
                    ins.then_inc(SM["t_xqT"], 1)
            xqT(0)

            def hqT(k, b):
                t.wait_ge(SM["v_hq"], 2 * k + (1 if b < 4 else 2))
                if b < 2 and k + 1 < ntot_t:
                    t.wait_ge(SM["c_xqfm"], 2 * (k + 1) + 1 + b)
                elif k * 8 + b >= 2:
                    t.wait_ge(SM["c_hqfm"], k * 8 + b - 1)
                for kk in range(4):
                    j = b * 4 + kk
                    ins = t.matmul(P[6 + b % 2][:, kk * 128:(kk + 1) * 128],
                                   hq_tm[:, j * 128:(j + 1) * 128], c_id[:],
                                   start=True, stop=True)
                ins.then_inc(SM["t_hqT"], 1)

            for i in range(nt):
                for s in range(NSUB):
                    k = i * NSUB + s
                    t.wait_ge(SM["c_xqfm"], 2 * (k + 1))
                    for pc in range(8):
                        jg = k * 8 + pc
                        if jg >= 3:
                            t.wait_ge(SM["c_g"], jg - 2)
                        for m in range(8):
                            ins = t.matmul(P[[2, 3, 0][jg % 3]][:],
                                           xq_fm[k % 2][:, m * 128:(m + 1) * 128],
                                           wgq[:, m * FI + pc * 512: m * FI + (pc + 1) * 512],
                                           start=(m == 0), stop=(m == 7))
                        ins.then_inc(SM["t_g"], 1)
                        if jg >= 3:
                            t.wait_ge(SM["c_u"], jg - 2)
                        for m in range(8):
                            ins = t.matmul(P[[4, 5, 1][jg % 3]][:],
                                           xq_fm[k % 2][:, m * 128:(m + 1) * 128],
                                           wuq[:, m * FI + pc * 512: m * FI + (pc + 1) * 512],
                                           start=(m == 0), stop=(m == 7))
                        ins.then_inc(SM["t_u"], 1)
                        if k >= 1:
                            hqT(k - 1, pc)
                    if k + 1 < ntot_t:
                        xqT(k + 1)
            for b in range(8):
                hqT(ntot_t - 1, b)

            t.wait_ge(SM["s_wb2"], 16)
            for i in range(nt):
                for s in range(NSUB):
                    k = i * NSUB + s
                    t.wait_ge(SM["s_hqia" if k % 2 == 0 else "s_hqib"], 16 * (k // 2 + 1))
                    for pc in range(2):
                        j = k * 2 + pc
                        if j >= 4:
                            t.wait_ge(SM["v_o"], j - 3)
                        for m in range(32):
                            ins = t.matmul(P[j % 4][:],
                                           hq_in[k % 2][:, m * 128:(m + 1) * 128],
                                           wdq[:, m * 1024 + pc * 512: m * 1024 + (pc + 1) * 512],
                                           start=(m == 0), stop=(m == 31))
                        ins.then_inc(SM["t_d"], 1)

        def scalar_body(sc):
            for j in range(NPC):
                _, _, w = piece_desc(j)
                sc.wait_ge(SM[f"s_stg{j % 2}"], 16 * (j // 2 + 1))
                if j > 0:
                    sc.drain()
                sc.activation(qout[0][:, 0:w], stage[j % 2][:, 0:w], AF.Abs,
                              accum_out=acc_sb[:, j:j + 1]).then_inc(SM["c_abs"], 1)

            def c_sqrt(i):
                sc.wait_ge(SM["v_st1"], i + 1)
                sc.activation(lns[:, 24:28], lns[:, 20:24], AF.Sqrt).then_inc(SM["c_sq"], 1)
            c_sqrt(0)
            for i in range(nt):

                for b in range(8):
                    s2, q = b // 2, b % 2
                    sc.wait_ge(SM["t_xnT"], i * 8 + b + 1)
                    dst = xn_fm[:].rearrange("p (mc t) -> p mc t", mc=8)[
                        :, q * 4:(q + 1) * 4, s2 * 128:(s2 + 1) * 128]
                    srcv = P[[2, 3, 0, 1][b % 4]][:].rearrange("p (mc4 t) -> p mc4 t", mc4=4)
                    sc.copy(dst, srcv).then_inc(SM["c_xnfm"], 1)
                for c in range(24):
                    sc.wait_ge(SM["t_qkv"], i * 24 + c + 1)
                    src = P[2 + c % 2][:].rearrange("p (G tb) -> p G tb", G=NGRP)
                    if c < 8:
                        dst = qt[:].rearrange("p (G c3 tb) -> p G c3 tb", G=NGRP, c3=8)[:, :, c, :]
                        ins = sc.mul(dst, src, 0.125)
                    elif c < 16:
                        dst = kt[:].rearrange("p (G c3 tb) -> p G c3 tb", G=NGRP, c3=8)[:, :, c - 8, :]
                        ins = sc.copy(dst, src)
                    else:
                        dst = vt[:].rearrange("p (G c3 tb) -> p G c3 tb", G=NGRP, c3=8)[:, :, c - 16, :]
                        ins = sc.copy(dst, src)
                    ins.then_inc(SM["c_qkvev"], 1)
                if i + 1 < nt:
                    c_sqrt(i + 1)

                def sc_atev(bb):
                    ib = i * NBAT + bb
                    BA = P[6] if (bb % 2 == 0 or bb == NBAT - 1) else P[0]
                    sc.wait_ge(SM["t_At"], ib + 1)
                    if ib >= 4:
                        sc.wait_ge(SM["t_AV"], ib - 3)
                    sc.copy(atsb[bb % 4][:], BA[:]).then_inc(SM["c_Atev"], 1)

                def sc_avev(bb):
                    ib = i * NBAT + bb
                    BV = P[7] if (bb % 2 == 0 or bb == NBAT - 1) else P[1]
                    sc.wait_ge(SM["t_AV"], ib + 1)
                    srcv = BV[:, 256:512].rearrange("p (g4 c8 tb) -> p g4 c8 tb", g4=4, c8=8)
                    dst = attn_fm[:].rearrange("p (c8 t) -> p c8 t", c8=8)[
                        :, :, bb * 32:(bb + 1) * 32].rearrange("p c8 (g4 tb) -> p g4 c8 tb", g4=4)
                    sc.copy(dst, srcv).then_inc(SM["c_AVev"], 1)

                for bb in range(NBAT):
                    ib = i * NBAT + bb
                    BV = P[7] if (bb % 2 == 0 or bb == NBAT - 1) else P[1]
                    mxc = 60 if bb % 2 == 0 else 48
                    if bb >= 3:
                        sc_atev(bb - 3)
                    acc = 64 if bb % 2 == 0 else 68
                    sc.wait_ge(SM["v_S2"], ib + 1)
                    for g4 in range(4):
                        ins = sc.activation(esb[bb % 2][:, g4 * 128:(g4 + 1) * 128],
                                            s1f[bb % 2][:, g4 * 128:(g4 + 1) * 128], AF.Exp,
                                            bias=lns[:, mxc + g4:mxc + g4 + 1],
                                            accum_out=lns[:, acc + g4:acc + g4 + 1])
                    ins.then_inc(SM["c_E"], 1)
                    sc.wait_ge(SM["t_V8"], ib + 1)
                    if ib >= 4:
                        sc.wait_ge(SM["t_AV"], ib - 3)
                    sc.copy(v8sb[bb % 4][:], BV[:, 0:256]).then_inc(SM["c_V8ev"], 1)
                    if bb >= 4:
                        sc_avev(bb - 4)
                sc_atev(NBAT - 3)
                sc_avev(NBAT - 4)
                sc_atev(NBAT - 2)
                sc_avev(NBAT - 3)
                sc_atev(NBAT - 1)
                sc_avev(NBAT - 2)
                sc_avev(NBAT - 1)

            ntot = nt * NSUB

            def sq2(kq):
                L = (kq % 2) * 48
                sc.wait_ge(SM["v_st2"], kq + 1)
                sc.activation(ln2[:, L + 3:L + 4], ln2[:, L + 2:L + 3], AF.Sqrt).then_inc(SM["c_sq2"], 1)
            sq2(0)
            if ntot > 1:
                sq2(1)

            def sc_hqev(kq, b):
                sc.wait_ge(SM["t_hqT"], kq * 8 + b + 1)
                if kq >= 2:
                    sc.wait_ge(SM[f"s_hqo{kq % 2}"], 16 * (kq // 2))
                sc.copy(hq_fm[kq % 2][:, b * 512:(b + 1) * 512],
                        P[6 + b % 2][:]).then_inc(SM["c_hqfm"], 1)

            def xq_evacs(k):
                if k >= 2:
                    sc.wait_ge(SM["t_u"], 8 * (k - 1))
                for b in range(2):
                    sc.wait_ge(SM["t_xqT"], k * 2 + b + 1)
                    sc.copy(xq_fm[k % 2][:, b * 512:(b + 1) * 512], P[6 + b][:]).then_inc(SM["c_xqfm"], 1)

            xq_evacs(0)
            for i in range(nt):
                for s in range(NSUB):
                    k = i * NSUB + s
                    if k + 2 < ntot:
                        sq2(k + 2)
                    for pc in range(8):
                        jg = k * 8 + pc
                        sc.wait_ge(SM["t_g"], jg + 1)
                        if jg >= 2:
                            sc.wait_ge(SM["v_M"], jg - 1)
                        sc.activation(silug[jg % 2][:], P[[2, 3, 0][jg % 3]][:], AF.Silu,
                                      scale=agm[:, k % 4:k % 4 + 1]).then_inc(SM["c_g"], 1)
                        sc.wait_ge(SM["t_u"], jg + 1)
                        sc.copy(u_sb[jg % 2][:], P[[4, 5, 1][jg % 3]][:]).then_inc(SM["c_u"], 1)
                        if k >= 1:
                            sc_hqev(k - 1, pc)
                    if k + 1 < ntot:
                        xq_evacs(k + 1)
            for b in range(8):
                sc_hqev(ntot - 1, b)

        def vector_body(v):
            TS = v.tensor_scalar
            TT = v.tensor_tensor
            STT = v.scalar_tensor_tensor

            v.wait_ge(SM["t_sums"], 1)
            v.tensor_copy(sums_sb[0:1, 0:NPC], P[0][0:1, 0:NPC])
            v.drain()
            v.reduce_sum(sums_sb[0:1, 48:49], sums_sb[0:1, 0:8], axis=AX.X)
            v.reduce_sum(sums_sb[0:1, 49:50], sums_sb[0:1, 8:16], axis=AX.X)
            v.reduce_sum(sums_sb[0:1, 50:51], sums_sb[0:1, 16:24], axis=AX.X)
            v.drain()
            TS(sums_sb[0:1, 51:53], sums_sb[0:1, 48:50], 1.0 / (D * FI), 1e-5, ALU.mult, ALU.max)
            TS(sums_sb[0:1, 53:54], sums_sb[0:1, 50:51], 1.0 / (FI * D), 1e-5, ALU.mult, ALU.max)
            v.drain()
            v.reciprocal(sums_sb[0:1, 8:11], sums_sb[0:1, 51:54]).then_inc(SM["v_scales"], 1)
            v.wait_ge(SM["g_bc"], 1)
            v.tensor_copy(wsc[:, 0:3], P[1][:, 0:3])
            v.drain()
            for j in range(NPC):
                _, _, w = piece_desc(j)
                v.wait_ge(SM["s_st2a" if j % 2 == 0 else "s_st2b"], 16 * (j // 2 + 1))
                if j >= 2:
                    v.wait_ge(SM[f"s_qo{j % 2}"], 16 * (j // 2))
                TS(qtmp[:, 0:w], stage[j % 2][:, 0:w], wsc[:, sw_col(j):sw_col(j) + 1],
                   MAGIC, ALU.mult, ALU.add).then_inc(SM["v_stg"], 1)
                v.drain()
                TS(qtmp[:, 0:w], qtmp[:, 0:w], MAGIC, None, ALU.subtract)
                v.drain()
                TS(qout[j % 2][:, 0:w], qtmp[:, 0:w], -1.0, 1.0, ALU.max, ALU.min).then_inc(SM["v_q"], 1)
                v.drain()

            def v_ln(i):
                xb = x_sb[i % 2]
                ro = 36 if i % 2 == 0 else 44
                v.wait_ge(SM[f"s_x{i % 2}"], 16 * (i // 2 + 1))
                for s2 in range(NSUB):
                    xs = xb[:, s2 * 1024:(s2 + 1) * 1024]
                    if s2 > 0:
                        v.drain()
                    v.bn_stats(lns[:, 0:6], xs[:, 0:512])
                    v.bn_stats(lns[:, 6:12], xs[:, 512:1024])
                    v.drain()
                    v.bn_aggr(lns[:, 12 + 2 * s2:14 + 2 * s2], lns[:, 0:12])
                v.drain()
                pairs = lns[:, 12:20].rearrange("p (s k) -> p s k", k=2)
                v.tensor_copy(lns[:, 52:56].rearrange("p s -> p s ()"), pairs[:, :, 0:1])
                v.tensor_copy(lns[:, 56:60].rearrange("p s -> p s ()"), pairs[:, :, 1:2])
                v.drain()
                TS(lns[:, 20:24], lns[:, 56:60], EPS, None, ALU.add).then_inc(SM["v_st1"], 1)
                v.wait_ge(SM["c_sq"], i + 1)
                v.reciprocal(lns[:, 28:32], lns[:, 24:28])
                v.drain()
                TT(lns[:, 32:36], lns[:, 28:32], lns[:, 28:32], ALU.mult)
                v.drain()
                TT(lns[:, 60:64], lns[:, 32:36], lns[:, 20:24], ALU.mult)
                v.drain()
                TS(lns[:, 32:36], lns[:, 60:64], -0.5, 1.5, ALU.mult, ALU.add)
                v.drain()
                TT(lns[:, ro:ro + 4], lns[:, 28:32], lns[:, 32:36], ALU.mult)
                v.drain()
                STT(lns[:, ro + 4:ro + 8], lns[:, 52:56], -1.0, lns[:, ro:ro + 4], ALU.mult, ALU.mult)
                v.drain()
                if i >= 1:
                    v.wait_ge(SM["t_xnT"], 8 * i)
                for s2 in range(NSUB):
                    ins = TS(xn_tm[:, s2 * 1024:(s2 + 1) * 1024],
                             xb[:, s2 * 1024:(s2 + 1) * 1024],
                             lns[:, ro + s2:ro + 1 + s2],
                             lns[:, ro + 4 + s2:ro + 5 + s2], ALU.mult, ALU.add)
                    if s2 == NSUB - 1:
                        ins.then_inc(SM["c_xn"], 1)

            v_ln(0)
            for i in range(nt):
                xb = x_sb[i % 2]
                if i + 1 < nt:
                    v_ln(i + 1)
                for pc in range(8):
                    v.wait_ge(SM["t_ksw"], i * 8 + pc + 1)
                    v.tensor_copy(ktx[:, pc * 512:(pc + 1) * 512], P[2 + pc % 2][:]).then_inc(SM["v_ksw"], 1)
                def v_s1(bb):
                    ib = i * NBAT + bb
                    Sb = P[4 + bb % 2]
                    mx = slice(60, 64) if bb % 2 == 0 else slice(48, 52)
                    v.wait_ge(SM["t_S"], ib + 1)
                    if ib >= 2:
                        v.wait_ge(SM["c_E"], ib - 1)
                    TT(s1f[bb % 2][:], Sb[:], c_mask[:], ALU.add)
                    v.drain()
                    v.reduce_max(lns[:, mx], s1f[bb % 2][:].rearrange("p (g c) -> p g c", g=4),
                                 axis=AX.X, negate=True).then_inc(SM["v_S2"], 1)

                def v_s2(bb):
                    ib = i * NBAT + bb
                    sm = slice(56, 60) if bb % 2 == 0 else slice(52, 56)
                    v.wait_ge(SM["c_E"], ib + 1)
                    acc2 = slice(64, 68) if bb % 2 == 0 else slice(68, 72)
                    v.reciprocal(lns[:, sm], lns[:, acc2])
                    v.drain()
                    if ib >= 4:
                        v.wait_ge(SM["t_At"], ib - 3)
                    TT(asb[bb % 4][:].rearrange("p (g c) -> p g c", g=4),
                       esb[bb % 2][:].rearrange("p (g c) -> p g c", g=4),
                       lns[:, sm].rearrange("p g -> p g ()").broadcast_to([128, 4, 128]),
                       ALU.mult).then_inc(SM["v_A"], 1)

                def v_pevac(s, pc):
                    j = s * 2 + pc
                    v.wait_ge(SM["t_pj"], i * 8 + j + 1)
                    if i == 1 and j == 0:
                        v.wait_ge(SM["s_qo0"], 16 * (NPC // 2))
                        v.wait_ge(SM["s_qo1"], 16 * (NPC // 2))
                    if i >= 2 and j == 0:
                        v.wait_ge(SM[f"s_xno{i % 2}"], 16 * (i // 2))
                    STT(xnew_sb[i % 2][:, s * 1024 + pc * 512: s * 1024 + (pc + 1) * 512],
                        (P[4 + pc] if s == 3 else P[2 + j % 2])[:], 1.0,
                        xb[:, s * 1024 + pc * 512: s * 1024 + (pc + 1) * 512],
                        ALU.mult, ALU.add).then_inc(SM["v_pe"], 1)

                for bb in range(NBAT):
                    v_s1(bb)
                    if bb >= 1:
                        v_s2(bb - 1)
                    if bb == 10:
                        v_pevac(0, 0)
                        v_pevac(0, 1)
                    elif bb == 13:
                        v_pevac(1, 0)
                        v_pevac(1, 1)
                v_s2(NBAT - 1)
                v_pevac(2, 0)
                v_pevac(2, 1)
                v_pevac(3, 0)
                v_pevac(3, 1)

            ntot = nt * NSUB

            def b1_stats(k):
                L = (k % 2) * 48
                xs = xnsub[k % 2]
                v.wait_ge(SM[f"s_xni{k % 2}"], 16 * (k // 2 + 1))
                if k >= 2:
                    v.wait_ge(SM["c_sq2"], k - 1)
                v.drain()
                v.bn_stats(ln2[:, L + 20:L + 26], xs[:, 0:512])
                v.bn_stats(ln2[:, L + 26:L + 32], xs[:, 512:1024])
                v.drain()
                v.bn_aggr(ln2[:, L + 0:L + 2], ln2[:, L + 20:L + 32])
                v.drain()
                TS(ln2[:, L + 2:L + 3], ln2[:, L + 1:L + 2], EPS, None, ALU.add)
                v.reduce_max(ln2[:, L + 7:L + 8], xs[:], axis=AX.X)
                v.tensor_reduce(ln2[:, L + 8:L + 9], xs[:], AX.X, ALU.min).then_inc(SM["v_st2"], 1)

            def b1_rest(k):
                L = (k % 2) * 48
                xs = xnsub[k % 2]
                v.wait_ge(SM["c_sq2"], k + 1)
                v.reciprocal(ln2[:, L + 4:L + 5], ln2[:, L + 3:L + 4])
                v.drain()
                TT(ln2[:, L + 5:L + 6], ln2[:, L + 4:L + 5], ln2[:, L + 4:L + 5], ALU.mult)
                v.drain()
                TT(ln2[:, L + 18:L + 19], ln2[:, L + 5:L + 6], ln2[:, L + 2:L + 3], ALU.mult)
                v.drain()
                TS(ln2[:, L + 19:L + 20], ln2[:, L + 18:L + 19], -0.5, 1.5, ALU.mult, ALU.add)
                v.drain()
                TT(ln2[:, L + 6:L + 7], ln2[:, L + 4:L + 5], ln2[:, L + 19:L + 20], ALU.mult)
                v.drain()
                TT(ln2[:, L + 9:L + 10], ln2[:, L + 7:L + 8], ln2[:, L + 0:L + 1], ALU.subtract)
                TT(ln2[:, L + 10:L + 11], ln2[:, L + 0:L + 1], ln2[:, L + 8:L + 9], ALU.subtract)
                v.drain()
                TT(ln2[:, L + 11:L + 12], ln2[:, L + 9:L + 10], ln2[:, L + 10:L + 11], ALU.max)
                v.drain()
                TT(ln2[:, L + 12:L + 13], ln2[:, L + 11:L + 12], ln2[:, L + 6:L + 7], ALU.mult)
                v.drain()
                TS(ln2[:, L + 13:L + 14], ln2[:, L + 12:L + 13], 1e-5, None, ALU.max)
                v.drain()
                v.reciprocal(ln2[:, L + 14:L + 15], ln2[:, L + 13:L + 14])
                v.drain()
                TS(ln2[:, L + 15:L + 16], ln2[:, L + 14:L + 15], 127.0, None, ALU.mult)
                v.drain()
                TT(ln2[:, L + 16:L + 17], ln2[:, L + 6:L + 7], ln2[:, L + 15:L + 16], ALU.mult)
                v.drain()
                STT(ln2[:, L + 17:L + 18], ln2[:, L + 0:L + 1], -1.0, ln2[:, L + 16:L + 17], ALU.mult, ALU.mult)
                v.drain()
                TS(ln2[:, L + 10:L + 11], ln2[:, L + 17:L + 18], 1.0, MAGIC, ALU.mult, ALU.add)
                if k >= 4:
                    v.wait_ge(SM["c_g"], 8 * (k - 3))
                TT(ln2[:, L + 18:L + 19], ln2[:, L + 15:L + 16], wsc[:, 0:1], ALU.mult)
                TT(ln2[:, L + 19:L + 20], ln2[:, L + 15:L + 16], wsc[:, 1:2], ALU.mult)
                v.drain()
                v.reciprocal(agm[:, k % 4:k % 4 + 1], ln2[:, L + 18:L + 19])
                v.reciprocal(agm[:, 4 + k % 4:5 + k % 4], ln2[:, L + 19:L + 20])
                if k >= 3:
                    v.wait_ge(SM["t_xqT"], 2 * (k - 2))
                TS(xqm[:], xs[:], ln2[:, L + 16:L + 17], ln2[:, L + 10:L + 11], ALU.mult, ALU.add)
                v.drain()
                TS(xq_tm[k % 3][:], xqm[:], MAGIC, None, ALU.subtract).then_inc(SM["v_xq"], 1)

            def b1_m(k):
                L = (k % 2) * 48
                for pc in range(8):
                    jg = k * 8 + pc
                    v.wait_ge(SM["c_u"], jg + 1)
                    v.wait_ge(SM["c_g"], jg + 1)
                    STT(m_sb[:, pc * 512:(pc + 1) * 512], u_sb[jg % 2][:],
                        agm[:, 4 + k % 4:5 + k % 4], silug[jg % 2][:],
                        ALU.mult, ALU.mult).then_inc(SM["v_M"], 1)
                    v.drain()
                    v.reduce_max(ln2[:, L + 32 + pc:L + 33 + pc],
                                 m_sb[:, pc * 512:(pc + 1) * 512],
                                 axis=AX.X, apply_absolute_value=True)
                v.drain()
                v.reduce_max(ln2[:, L + 40:L + 41], ln2[:, L + 32:L + 40], axis=AX.X)
                v.drain()
                TS(ln2[:, L + 41:L + 42], ln2[:, L + 40:L + 41], 1e-5, None, ALU.max)
                v.drain()
                v.reciprocal(ln2[:, L + 42:L + 43], ln2[:, L + 41:L + 42])
                v.drain()
                TS(ln2[:, L + 43:L + 44], ln2[:, L + 42:L + 43], 127.0, None, ALU.mult)
                v.drain()
                TT(ln2[:, L + 44:L + 45], ln2[:, L + 43:L + 44], wsc[:, 2:3], ALU.mult)
                v.drain()
                v.reciprocal(gam[:, k:k + 1], ln2[:, L + 44:L + 45])
                if k >= 1:
                    v.wait_ge(SM["t_hqT"], 8 * k)
                for hh in range(2):
                    hs = slice(hh * 2048, (hh + 1) * 2048)
                    TS(m_sb[:, hs], m_sb[:, hs], ln2[:, L + 43:L + 44], MAGIC, ALU.mult, ALU.add)
                    v.drain()
                    TS(hq_tm[:, hs], m_sb[:, hs], MAGIC, None, ALU.subtract).then_inc(SM["v_hq"], 1)
                    v.drain()

            b1_stats(0)
            b1_rest(0)
            if ntot > 1:
                b1_stats(1)
                b1_rest(1)
            for k in range(ntot):
                if k + 2 < ntot:
                    b1_stats(k + 2)
                    b1_rest(k + 2)
                b1_m(k)

            for i in range(nt):
                for s in range(NSUB):
                    k = i * NSUB + s
                    v.wait_ge(SM["s_xnia" if k % 2 == 0 else "s_xnib"], 16 * (k // 2 + 1))
                    for pc in range(2):
                        j = k * 2 + pc
                        v.wait_ge(SM["t_d"], j + 1)
                        if k >= 2 and pc == 0:
                            v.wait_ge(SM[f"s_out{k % 2}"], 16 * (k // 2))
                        STT(out_sb[k % 2][:, pc * 512:(pc + 1) * 512],
                            P[j % 4][:], gam[:, k:k + 1],
                            xnsub2[k % 2][:, pc * 512:(pc + 1) * 512],
                            ALU.mult, ALU.add).then_inc(SM["v_o"], 1)

        block.sync(sync_body)
        block.gpsimd(gpsimd_body)
        block.tensor(tensor_body)
        block.scalar(scalar_body)
        block.vector(vector_body)

    return nc


def _host_layouts(Wqkv, Wproj, bproj, w_gate, w_up, w_down):
    H, DH = 16, 64
    perm = np.empty(3 * D, dtype=np.int64)
    idx = 0
    for blk, off in ((0, 0), (1, 64), (2, 128)):
        for c in range(8):
            for par in range(2):
                h = 2 * c + par
                base = h * 192 + off
                perm[idx:idx + DH] = np.arange(base, base + DH)
                idx += DH
    wqkvT = np.ascontiguousarray(Wqkv[perm].T)
    hd_perm = np.empty(D, dtype=np.int64)
    idx = 0
    for c in range(8):
        for par in range(2):
            h = 2 * c + par
            hd_perm[idx:idx + DH] = np.arange(h * DH, (h + 1) * DH)
            idx += DH
    wprojT = np.ascontiguousarray(Wproj.T[hd_perm])
    bproj2 = np.ascontiguousarray(bproj.reshape(1, D))
    wgT = np.ascontiguousarray(w_gate.T)
    wuT = np.ascontiguousarray(w_up.T)
    wdT = np.ascontiguousarray(w_down.T)
    return wqkvT, wprojT, bproj2, wgT, wuT, wdT


def kernel(L0_attn_x, L0_attn_Wqkv, L0_attn_Wproj, L0_attn_bproj,
           L0_ffn_w_gate, L0_ffn_w_up, L0_ffn_w_down):
    x = np.ascontiguousarray(np.asarray(L0_attn_x, dtype=np.float32))
    B, N, _ = x.shape
    xf = x.reshape(B * N, D)
    wqkvT, wprojT, bproj2, wgT, wuT, wdT = _host_layouts(
        np.asarray(L0_attn_Wqkv, np.float32), np.asarray(L0_attn_Wproj, np.float32),
        np.asarray(L0_attn_bproj, np.float32), np.asarray(L0_ffn_w_gate, np.float32),
        np.asarray(L0_ffn_w_up, np.float32), np.asarray(L0_ffn_w_down, np.float32))
    per_core = (B * N) // N_CORES
    nt = per_core // TOK
    nc = build_nc(nt)
    in_maps = []
    for c in range(N_CORES):
        in_maps.append({
            "x": np.ascontiguousarray(xf[c * per_core:(c + 1) * per_core]),
            "wqkvT": wqkvT, "wprojT": wprojT, "bproj": bproj2,
            "wgT": wgT, "wuT": wuT, "wdT": wdT,
        })
    res = run_bass_kernel_spmd(nc, in_maps, core_ids=list(range(N_CORES)))
    out = np.concatenate([res.results[c]["out"] for c in range(N_CORES)], axis=0)
    return out.reshape(B, N, D).astype(np.float32)
